# version 09
# baseline (speedup 1.0000x reference)
import numpy as np
import ml_dtypes
from contextlib import ExitStack
from collections import defaultdict

import concourse.bass as bass
import concourse.mybir as mybir
from concourse.bass_utils import run_bass_kernel_spmd

F32 = mybir.dt.float32
BF16 = mybir.dt.bfloat16
U8 = mybir.dt.uint8
AF = mybir.ActivationFunctionType
ALU = mybir.AluOpType
BF = ml_dtypes.bfloat16

C = 2048
NCH = C // 128
T = 1024
B = 4
HID0 = 5632
NH0 = HID0 // 128
INTER = 8192
NHI = INTER // 128
NHEAD = 16
LS = 64
NCORES = 8

ENGINES = ("tensor", "vector", "scalar", "gpsimd", "sync")


def pack_lhsT(Wl):
    K, M = Wl.shape
    W4 = Wl.reshape(K // 128, 128, M // 128, 128)
    return np.ascontiguousarray(W4.transpose(2, 1, 0, 3))


def pack_rhs(Wr, slab):
    K, N = Wr.shape
    W4 = Wr.reshape(K // 128, 128, N // slab, slab)
    return np.ascontiguousarray(W4.transpose(2, 1, 0, 3))


def vec_fm(v):
    n = v.shape[0]
    return np.ascontiguousarray(v.reshape(n // 128, 128).T)


def hadamard(n):
    H = np.array([[1.0]], dtype=np.float32)
    while H.shape[0] < n:
        H = np.block([[H, H], [H, -H]])
    return H


AUTO_SEM = {"vector": "ve", "scalar": "ac", "gpsimd": "po"}


class Prog:
    def __init__(self):
        self.items = {e: [] for e in ENGINES}
        self.ctr = defaultdict(int)
        self.sem_names = set()

    def add(self, eng, fn, waits=(), inc=None, auto=True):
        if auto and fn is not None and eng in AUTO_SEM:
            s = AUTO_SEM[eng]
            waits = tuple(waits) + ((s, self.ctr[s]),)
            if inc is None:
                inc = (s, 1)
            else:
                assert inc[0] == s, (eng, inc)
        w = tuple((s, v) for (s, v) in waits if v > 0)
        for s, _ in w:
            self.sem_names.add(s)
        r = None
        if inc is not None:
            self.sem_names.add(inc[0])
            self.ctr[inc[0]] += inc[1]
            r = self.ctr[inc[0]]
        self.items[eng].append((fn, w, inc))
        return r

    def v(self, sem):
        return self.ctr[sem]

    def emit(self, eng_name, eng, sems):
        waited = defaultdict(int)
        for fn, waits, inc in self.items[eng_name]:
            for s, val in waits:
                if waited[s] < val:
                    eng.wait_ge(sems[s], val)
                    waited[s] = val
            instr = fn(eng) if fn is not None else None
            if inc is not None:
                instr.then_inc(sems[inc[0]], inc[1])


class MK:
    def __init__(self, n_sub=4):
        self.n_sub = n_sub
        self.nc = bass.Bass("TRN2", target_bir_lowering=False)
        self.p = Prog()
        self.stack = None

    def sb(self, name, shape, dt, stack=None):
        st = stack if stack is not None else self.stack
        return st.enter_context(self.nc.sbuf_tensor(name, list(shape), dt))

    _scr_id = 0

    def alloc_scratch(self, sc, big=True):
        i = MK._scr_id
        MK._scr_id += 1
        self.r_tile = self.sb(f"r_tile{i}", (128, T + 1), F32, sc)
        self.mr_tile = self.sb(f"mr_tile{i}", (128, T + 1), F32, sc)
        self.scr_a = self.sb(f"scr_a{i}", (128, T + 1), F32, sc)
        self.xbf = self.sb(f"xbf{i}", (128, T + 1), BF16, sc)
        self.xbf2 = self.sb(f"xbfb{i}", (128, T + 1), BF16, sc)
        self.x2bf2 = self.sb(f"x2bfb{i}", (128, T + 1), BF16, sc)
        if big:
            self.scr_b = self.sb(f"scr_b{i}", (128, T), F32, sc)
            self.scr_c = self.sb(f"scr_c{i}", (128, T), F32, sc)

    def dma(self, eng, fn, sem, waits=()):
        prev = self.p.v(sem)
        return self.p.add(eng, fn, waits=tuple(waits) + ((sem, prev),),
                          inc=(sem, 16), auto=False)

    def din(self, name, shape, dt=F32):
        return self.nc.dram_tensor(name, list(shape), dt, kind="ExternalInput").ap()

    def dint(self, name, shape, dt=F32):
        return self.nc.dram_tensor(name, list(shape), dt).ap()

    def build(self):
        nc, p = self.nc, self.p
        n_sub = self.n_sub

        self.x_in = self.din("x_fm", (C, T + 1))
        self.wk_p = self.din("wk_p", (NCH, 128, NCH, 128))
        self.wv_p = self.din("wv_p", (NCH, 128, NCH, 128))
        self.wr_p = self.din("wr_p", (NCH, 128, NCH, 128))
        self.wo_p = self.din("wo_p", (NCH, 128, NCH, 128))
        self.vecs = self.din("vecs", (128, 128))
        self.vecs2_d = self.din("vecs2", (128, 128))
        self.wg_p = self.din("wg_p", (NH0, 128, NCH, 128), U8)
        self.wu_p = self.din("wu_p", (NH0, 128, NCH, 128), U8)
        self.wd_p = self.din("wd_p", (NCH, 128, NH0, 128), U8)
        self.jramp1_d = self.din("jramp1", (128, T))
        self.msel_d = self.din("msel", (128, 8))
        self.wq1_p = self.din("wq1_p", (NCH, 128, NCH, 128))
        self.wo1_p = self.din("wo1_p", (NCH, 128, NCH, 128))
        self.wf_p = self.din("wf_p", (8, 128, NCH, 256))
        self.wi_p = self.din("wi_p", (8, 128, NCH, 256))
        self.vecs3_d = self.din("vecs3", (128, 16))
        self.ident_d = self.din("ident", (128, 128), BF16)
        self.u64bd_d = self.din("u64bd", (128, 128), BF16)
        self.mask64_d = self.din("mask64", (128, 64), BF16)
        self.VC = {"td": 0, "tf": 16, "tmk": 32, "tmv": 48, "tmr": 64,
                   "eps": 80, "c_zero": 81}

        self.out_fm = nc.dram_tensor("out_fm", [C, T], F32, kind="ExternalOutput").ap()

        self.resid_a = self.dint("resid_a", (C, T))
        self.resid_b = self.dint("resid_b", (C, T))
        self.nN_h = self.dint("nN_h", (C, T + 1), BF16)
        self.nD_h = self.dint("nD_h", (C, T + 1), BF16)
        self.cc0_in = self.dint("cc0_in", (128, NCH * 2))
        self.cc0_out = self.dint("cc0_out", (NCORES * 128, NCH * 2))
        self.cc1_in = self.dint("cc1_in", (128, C))
        self.cc1_out = self.dint("cc1_out", (NCORES * 128, C))
        self.wdn2_h = self.dint("wdn2_h", (NCH, 128, NHI, 128), BF16)
        self.qh_h = self.dint("qh_h", (C, T), BF16)
        self.ob_h = self.dint("ob_h", (C, T), BF16)
        self.wup_p = self.din("wup_p", (NHI, 128, NCH, 128))
        self.wgt_p = self.din("wgt_p", (NHI, 128, NCH, 128))
        self.wdn_z = self.din("wdn_z", (128, 64, C))
        self.h64i2_d = self.din("h64i2", (128, 128), BF16)
        self.h128_d = self.din("h128", (128, 128), BF16)
        self.vecs4_d = self.din("vecs4", (128, 256))

        with ExitStack() as top:
            self.stack = top
            self.PB = [top.enter_context(self.nc.psum_tensor(f"pb{i}", [128, 512], F32)) for i in range(7)]
            self.vec_sb = self.sb("vec_sb", (128, 128), F32)
            self.ones_bf = self.sb("ones_bf", (128, 128), BF16)
            self.msel_sb = self.sb("msel_sb", (128, 8), F32)
            self.xch = [self.sb(f"xch{i}", (128, T + 1), F32) for i in range(2)]
            self.x2bf = self.sb("x2bf", (128, T + 1), BF16)
            self.wsl = self.sb("wsl_arena", (128, 8192), BF16)

            self.h64i2_sb = self.sb("h64i2_sb", (128, 128), BF16)
            self.prep_dma_i = 0
            self.prep_dma_done = {}
            self.prep_ev = [0, 0]
            self.prep_st = [0, 0, 0, 0]
            self.dma("sync", lambda e: e.dma_start(out=self.vec_sb[:], in_=self.vecs[:]), "dvl")
            self.v_h64 = self.dma("sync", lambda e: e.dma_start(
                out=self.h64i2_sb[:], in_=self.h64i2_d[:]), "dvl")
            nld = self.dma("sync", lambda e: e.dma_start(out=self.msel_sb[:], in_=self.msel_d[:]), "dvl")
            self.v_vecld = nld
            p.add("vector", lambda e: e.memset(self.ones_bf[:], 1.0), inc=("ve", 1))
            self.v_ones = p.v("ve")

            if True:
                self.l0a()
            if n_sub >= 2:
                self.l0f()
            if n_sub >= 3:
                self.l1a()
            if n_sub >= 4:
                self.l1f()

            src = self.resid_a if (n_sub % 2 == 1) else self.resid_b
            bar = self.barrier() + [(sname, p.v(sname)) for sname in list(p.ctr)
                                    if sname.startswith("d")]
            stv = 0
            for a in range(NCH):
                stv = self.dma("sync",
                               (lambda aa, s_: lambda e: e.dma_start(
                                   out=self.out_fm[aa * 128:(aa + 1) * 128, :],
                                   in_=s_[aa * 128:(aa + 1) * 128, :]))(a, src),
                               "dst", waits=bar)
            p.add("sync", None, waits=[("dst", stv)])

            with ExitStack() as ss:
                sems = {s: ss.enter_context(nc.semaphore(s))
                        for s in sorted(p.sem_names)}
                blk = ss.enter_context(nc.Block())
                for name in ENGINES:
                    def mk(nm):
                        def f(eng):
                            p.emit(nm, eng, sems)
                        return f
                    getattr(blk, name)(mk(name))
        return nc

    def emit_prep_dmas(self, n):
        for _ in range(n):
            i = self.prep_dma_i
            if i >= 32:
                return
            self.prep_dma_i += 1
            buf = self.prep_tiles[i % 2]
            self.prep_dma_done[i] = self.dma(
                "gpsimd",
                (lambda ii, b_: lambda e: e.dma_start(
                    out=b_[:], in_=self.wdn_z[:, 2 * ii:2 * ii + 2, :]
                    .rearrange("p a q -> p (a q)")))(i, buf),
                "dwc", waits=[("pmm", max(0, i - 1))])

    def emit_prep_tile(self, i, PB):
        p = self.p
        for m in range(8):
            it = i * 8 + m
            p2 = 2 * i + (m // 4)
            ds = m % 4
            pb = PB[6 + (it % 2)]
            mm = p.add("tensor",
                  (lambda pb_, b_, m_: lambda e: e.matmul(
                      pb_[:], self.h64i2_sb[:],
                      b_[:, m_ * 512:(m_ + 1) * 512],
                      start=True, stop=True))(pb, self.prep_tiles[i % 2], m),
                  waits=[("dwc", self.prep_dma_done[i]),
                         ("ac", self.prep_ev[it % 2]), ("dvl", self.v_h64)],
                  inc=("pmm", 1) if m == 7 else ("mm", 1))
            mmw = ("pmm", p.v("pmm")) if m == 7 else ("mm", mm)
            stg = self.prep_stage[it % 4]
            p.add("scalar",
                  (lambda pb_, st_: lambda e: e.activation(
                      st_[:], pb_[:], AF.Copy))(pb, stg),
                  waits=[mmw, ("dwd%d" % (it % 4), self.prep_st[it % 4])],
                  inc=("ac", 1))
            self.prep_ev[it % 2] = p.v("ac")
            va = p.v("ac")
            jd0 = ds * 4
            self.prep_st[it % 4] = self.dma(
                "sync",
                (lambda st_, p2_, jd0_: lambda e: e.dma_start(
                    out=self.wdn2_h[jd0_:jd0_ + 4, 2 * p2_:2 * p2_ + 2, :, :]
                    .rearrange("a b c d -> b c a d"),
                    in_=st_[:]))(stg, p2, jd0),
                "dwd%d" % (it % 4), waits=[("ac", va)])

    def barrier(self):
        p = self.p
        vals = [("bmm", p.add("tensor", lambda e: e.nop(),
                              waits=[("mm", p.v("mm"))], inc=("bmm", 1))),
                ("ve", p.add("vector", lambda e: e.nop())),
                ("ac", p.add("scalar", lambda e: e.nop())),
                ("po", p.add("gpsimd", lambda e: e.nop()))]
        return vals

    def emit_ln(self, src_hbm, ncols, y_chunks, with_y=True, rms=False,
                src_sb=None):
        nc, p = self.nc, self.p
        PB = self.PB
        xbf_l, r_tile_l, mr_tile_l, scr_a_l = (self.xbf, self.r_tile,
                                               self.mr_tile, self.scr_a)
        xbfp = [xbf_l, self.xbf2]
        x2bfp = [self.x2bf, self.x2bf2]
        nhalf = ncols // 512
        rem = ncols - nhalf * 512
        mm_chunk = [0] * NCH
        ld = [0] * NCH
        entry_ve, entry_ac = p.v("ve"), p.v("ac")
        psum_guard = [("ve", entry_ve), ("ac", entry_ac)]
        for a in range(NCH):
            if src_sb is None:
                buf = self.xch[a % 2]
                sem = "dxa" if a % 2 == 0 else "dxb"
                ld[a] = self.dma("sync",
                                 (lambda aa, b_: lambda e: e.dma_start(
                                     out=b_[:, 0:ncols],
                                     in_=src_hbm[aa * 128:(aa + 1) * 128, :]))(a, buf),
                                 sem, waits=[("ve", p.v("ve")),
                                             ("dst", p.v("dst"))])
            else:
                buf = src_sb[a]
                ld[a] = 0
            gate = [(("dxa" if a % 2 == 0 else "dxb"), ld[a])] if ld[a] else []
            if a >= 2:
                gate.append(("mm", mm_chunk[a - 2]))
            xb_, x2_ = xbfp[a % 2], x2bfp[a % 2]
            p.add("vector",
                  (lambda b_, xb2: lambda e: e.tensor_copy(
                      xb2[:, 0:ncols], b_[:, 0:ncols]))(buf, xb_),
                  waits=gate, inc=("ve", 1))
            vcast = p.v("ve")
            p.add("scalar",
                  (lambda xb2, x22: lambda e: e.activation(
                      x22[:, 0:ncols], xb2[:, 0:ncols], AF.Square))(xb_, x2_),
                  waits=[("ve", vcast)], inc=("ac", 1))
            vsq = p.v("ac")
            first, last = (a == 0), (a == NCH - 1)
            srcs = [(x2_, 2, vsq, "ac", True)]
            if not rms:
                srcs = [(xb_, 0, vcast, "ve", False)] + srcs
            nmm = len(srcs) * (nhalf + (1 if rem else 0))
            imm = 0
            for si, (srt, pb0, wv, ws, is_last_src) in enumerate(srcs):
                is_first_src = (si == 0)
                for hh in range(nhalf):
                    imm += 1
                    mm_chunk[a] = p.add("tensor",
                          (lambda pb_, sl_, f_, l_: lambda e: e.matmul(
                              pb_[:], self.ones_bf[:], sl_, start=f_, stop=l_))(
                              PB[pb0 + hh], srt[:, hh * 512:(hh + 1) * 512], first, last),
                          waits=[(ws, wv), ("ve", self.v_ones)] + psum_guard,
                          inc=("mm", 1) if imm == nmm else None)
                if rem:
                    imm += 1
                    mm_chunk[a] = p.add("tensor",
                          (lambda sl_, f_, l_, co_: lambda e: e.matmul(
                              PB[4][:, co_:co_ + 1], self.ones_bf[:], sl_,
                              start=f_, stop=l_))(
                              srt[:, nhalf * 512:ncols],
                              first and is_first_src, last and is_last_src,
                              pb0 // 2),
                          waits=psum_guard,
                          inc=("mm", 1) if imm == nmm else None)
        mm_stats = p.v("mm")

        def ts(dst, src, s1, s2, op0, op1=ALU.bypass):
            return lambda e: e.tensor_scalar(dst, src, s1, s2, op0, op1)

        vwaits = [("mm", mm_stats)]
        if not rms:
            for hh in range(nhalf):
                p.add("vector", ts(scr_a_l[:, hh * 512:(hh + 1) * 512], PB[hh][:],
                                   1.0 / C, None, ALU.mult), waits=vwaits, inc=("ve", 1))
            if rem:
                p.add("vector", ts(scr_a_l[:, nhalf * 512:ncols], PB[4][:, 0:1],
                                   1.0 / C, None, ALU.mult), inc=("ve", 1))
        for hh in range(nhalf):
            p.add("vector", ts(r_tile_l[:, hh * 512:(hh + 1) * 512], PB[2 + hh][:],
                               1.0 / C, None, ALU.mult), waits=vwaits, inc=("ve", 1))
        if rem:
            p.add("vector", ts(r_tile_l[:, nhalf * 512:ncols], PB[4][:, 1:2],
                               1.0 / C, None, ALU.mult), inc=("ve", 1))
        v_mv = p.v("ve")
        epssl = self.vec_sb[:, self.VC["eps"]:self.VC["eps"] + 1]
        if not rms:
            p.add("scalar",
                  lambda e: e.activation(mr_tile_l[:, 0:ncols],
                                         scr_a_l[:, 0:ncols], AF.Square),
                  waits=[("ve", v_mv)], inc=("ac", 1))
            p.add("vector",
                  lambda e: e.scalar_tensor_tensor(
                      r_tile_l[:, 0:ncols], mr_tile_l[:, 0:ncols], -1.0,
                      r_tile_l[:, 0:ncols], ALU.mult, ALU.add),
                  waits=[("ac", p.v("ac"))], inc=("ve", 1))
        v_var = p.v("ve")
        p.add("scalar",
              lambda e: e.activation(r_tile_l[:, 0:ncols],
                                     r_tile_l[:, 0:ncols], AF.Ln, bias=epssl),
              waits=[("ve", v_var), ("dvl", self.v_vecld)], inc=("ac", 1))
        p.add("scalar",
              lambda e: e.activation(r_tile_l[:, 0:ncols],
                                     r_tile_l[:, 0:ncols], AF.Exp, scale=-0.5),
              inc=("ac", 1))
        v_rstd = p.v("ac")
        if not rms:
            p.add("vector",
                  lambda e: e.tensor_mul(mr_tile_l[:, 0:ncols],
                                         scr_a_l[:, 0:ncols], r_tile_l[:, 0:ncols]),
                  waits=[("ac", v_rstd)], inc=("ve", 1))
        self.v_stats = p.v("ve")
        self.v_stats_ac = v_rstd

        if with_y:
            for a in range(NCH):
                buf = self.xch[a % 2]
                sem = "dxa" if a % 2 == 0 else "dxb"
                if src_sb is None:
                    ld2 = self.dma("sync",
                                   (lambda aa, b_: lambda e: e.dma_start(
                                       out=b_[:, 0:ncols],
                                       in_=src_hbm[aa * 128:(aa + 1) * 128, :]))(a, buf),
                                   sem, waits=[("ve", p.v("ve"))])
                else:
                    buf = src_sb[a]
                    ld2 = 0
                gate = ([(sem, ld2)] if ld2 else []) + [("ve", self.v_stats),
                                                        ("ac", v_rstd)]
                p.add("vector",
                      (lambda b_: lambda e: e.tensor_mul(
                          scr_a_l[:, 0:ncols], b_[:, 0:ncols],
                          r_tile_l[:, 0:ncols]))(buf),
                      waits=gate, inc=("ve", 1))
                p.add("vector",
                      (lambda aa: lambda e: e.tensor_sub(
                          y_chunks[aa][:, 0:ncols], scr_a_l[:, 0:ncols],
                          mr_tile_l[:, 0:ncols]))(a),
                      inc=("ve", 1))
        return p.v("ve")

    def emit_proj(self, wpack, nk, nm, rhs_fn, evac_fn, v_rhs_ready,
                  extra_k=None, wslab_off=0, wdt=None):
        p = self.p
        PB = self.PB
        kcols = nk * 128
        slabs = [self.wsl[:, wslab_off + i * kcols: wslab_off + (i + 1) * kcols]
                 for i in range(2)]
        mm0 = p.v("mm")
        slab_user = [mm0, mm0]
        bank_free = [(0, 0)] * 2
        wdone = [0] * nm
        for j in range(nm):
            sl = slabs[j % 2]
            wsem = "dwa" if j % 2 == 0 else "dwb"
            wdone[j] = self.dma("gpsimd",
                                (lambda jj, s_: lambda e: e.dma_start(
                                    out=s_, in_=wpack[jj].rearrange("p a q -> p (a q)")))(j, sl),
                                wsem, waits=[("mm", slab_user[j % 2])])
            b0 = 0 if j % 2 == 0 else 4
            bw = []
            if j >= 2:
                bw = [("ve", bank_free[j % 2][0]), ("ac", bank_free[j % 2][1])]
            mm_done = 0
            for half in range(2):
                pb = PB[b0 + half]
                for a in range(nk):
                    is_last = (half == 1 and a == nk - 1 and extra_k is None)
                    mm_done = p.add("tensor",
                          (lambda pb_, l_, r_, f_, st_: lambda e: e.matmul(
                              pb_[:], l_, r_, start=f_, stop=st_))(
                              pb, sl[:, a * 128:(a + 1) * 128], rhs_fn(a, half),
                              a == 0, (a == nk - 1) and extra_k is None),
                          waits=([(wsem, wdone[j]), ("ve", v_rhs_ready)] + bw)
                          if a == 0 else (),
                          inc=("mm", 1) if is_last else None) or mm_done
                if extra_k is not None:
                    mm_done = extra_k(j, pb, half, half == 1) or mm_done
            slab_user[j % 2] = mm_done
            bank_free[j % 2] = evac_fn(j, b0, mm_done)

    def l0a(self):
        nc, p = self.nc, self.p
        VC = self.VC

        with ExitStack() as sc:
            E = lambda n, s, d: self.sb(n, s, d, sc)
            PB = self.PB + [sc.enter_context(nc.psum_tensor("pb7a", [128, 512], F32))]
            self.alloc_scratch(sc)
            scr_a_l, scr_b_l, scr_c_l = self.scr_a, self.scr_b, self.scr_c
            y_bf = [E(f"y{a}", (128, T + 1), BF16) for a in range(NCH)]
            gA = [E(f"gA{a}", (128, T), BF16) for a in range(NCH)]
            gB = [E(f"gB{a}", (128, T), BF16) for a in range(NCH)]
            wkv = [E(f"wkv{a}", (128, T), BF16) for a in range(NCH)]
            jramp = E("jramp", (128, T), F32)
            w_vec = E("w_vec", (128, NCH), F32)
            lam_vec = E("lam_vec", (128, NCH), F32)
            eu_vec = E("eu_vec", (128, NCH), F32)
            eul_vec = E("eul_vec", (128, NCH), F32)
            ndp = [E(f"ndp{i}", (128, T + 1), BF16) for i in range(2)]
            p_f = [self.xch[0][:, 0:T], self.xch[1][:, 0:T]]
            pv_f = scr_b_l
            lamt = scr_c_l
            states = E("states", (128, NCH * 2), F32)
            self.prep_tiles = [E(f"prept{i}", (128, 4096), BF16) for i in range(2)]
            self.prep_stage = [self.xbf2[:, 0:512], self.xbf2[:, 512:1024],
                               self.x2bf2[:, 0:512], self.x2bf2[:, 512:1024]]
            shard = E("shard", (128, NCH * 2), F32)
            sN = E("sN", (128, NCH), F32)
            sD = E("sD", (128, NCH), F32)

            ldj = self.dma("sync", lambda e: e.dma_start(out=jramp[:], in_=self.jramp1_d[:]),
                           "dvl")

            tdsl = self.vec_sb[:, VC["td"]:VC["td"] + NCH]
            tfsl = self.vec_sb[:, VC["tf"]:VC["tf"] + NCH]
            p.add("scalar", lambda e: e.activation(w_vec[:], tdsl, AF.Exp),
                  waits=[("dvl", self.v_vecld)], inc=("ac", 1))
            p.add("vector", lambda e: e.tensor_scalar_mul(w_vec[:], w_vec[:], -1.0),
                  waits=[("ac", p.v("ac"))], inc=("ve", 1))
            vw = p.v("ve")
            p.add("scalar", lambda e: e.activation(lam_vec[:], w_vec[:], AF.Exp),
                  waits=[("ve", vw)], inc=("ac", 1))
            p.add("scalar", lambda e: e.activation(eu_vec[:], tfsl, AF.Exp),
                  inc=("ac", 1))
            vac = p.v("ac")
            p.add("vector", lambda e: e.tensor_mul(eul_vec[:], eu_vec[:], lam_vec[:]),
                  waits=[("ac", vac)], inc=("ve", 1))
            p.add("vector", lambda e: e.tensor_scalar(
                eul_vec[:], eul_vec[:], -1.0, 1.0, ALU.mult, ALU.add), inc=("ve", 1))

            self.emit_ln(self.x_in, T + 1, y_bf)

            for a in range(NCH):
                p.add("vector",
                      (lambda aa: lambda e: e.tensor_sub(
                          scr_b_l[:], y_bf[aa][:, 1:T + 1], y_bf[aa][:, 0:T]))(a),
                      inc=("ve", 1))
                for dst, col in ((gA, VC["tmk"]), (gB, VC["tmv"]),
                                 (wkv, VC["tmr"])):
                    sl = self.vec_sb[:, col + a:col + a + 1]
                    p.add("vector",
                          (lambda aa, d_, s_: lambda e: e.scalar_tensor_tensor(
                              d_[aa][:], scr_b_l[:], s_, y_bf[aa][:, 0:T],
                              ALU.mult, ALU.add))(a, dst, sl),
                          inc=("ve", 1))
            v_xkv = p.v("ve")

            p.add("vector", lambda e: e.memset(ndp[0][:, 0:1], 0.0))
            p.add("vector", lambda e: e.memset(ndp[1][:, 0:1], 0.0))

            kcols = C
            ksl = [self.wsl[:, i * kcols:(i + 1) * kcols] for i in range(2)]
            vsl = [self.wsl[:, (2 + i) * kcols:(3 + i) * kcols] for i in range(2)]
            slab_user = [0, 0, 0, 0]
            scans_done = [0] * NCH
            pexp_done = [0] * NCH
            pv_done = [0] * NCH
            nd_store = [0] * NCH
            for j in range(NCH):
                kb, vb = ksl[j % 2], vsl[j % 2]
                wsem = "dwa" if j % 2 == 0 else "dwb"
                wkd = self.dma("gpsimd",
                               (lambda jj, s_: lambda e: e.dma_start(
                                   out=s_, in_=self.wk_p[jj].rearrange("p a q -> p (a q)")))(j, kb),
                               wsem, waits=[("mm", slab_user[j % 2])])
                wvd = self.dma("gpsimd",
                               (lambda jj, s_: lambda e: e.dma_start(
                                   out=s_, in_=self.wv_p[jj].rearrange("p a q -> p (a q)")))(j, vb),
                               wsem, waits=[("mm", slab_user[2 + j % 2])])
                self.emit_prep_dmas(2)
                kb0 = 0 if j % 2 == 0 else 4
                vb0 = 2
                bwk = [("ac", pexp_done[j - 2]), ("ve", scans_done[j - 2])] if j >= 2 else []
                bwv = [("ve", pv_done[j - 1])] if j >= 1 else []
                for (pb0, wbuf, rhs, wwait, si, bw) in (
                        (kb0, kb, gA, wkd, j % 2, bwk),
                        (vb0, vb, gB, wvd, 2 + j % 2, bwv)):
                    for half in range(2):
                        pb = PB[pb0 + half]
                        for a in range(NCH):
                            is_last = (half == 1 and a == NCH - 1)
                            r = p.add("tensor",
                                  (lambda pb_, l_, r_, f_, st_: lambda e: e.matmul(
                                      pb_[:], l_, r_, start=f_, stop=st_))(
                                      pb, wbuf[:, a * 128:(a + 1) * 128],
                                      rhs[a][:, half * 512:(half + 1) * 512],
                                      a == 0, a == NCH - 1),
                                  waits=([(wsem, wwait), ("ve", v_xkv)] + bw)
                                  if a == 0 else (),
                                  inc=("mm", 1) if is_last else None)
                            if is_last:
                                slab_user[si] = r
                mm_k = p.v("mm") - 1
                mm_v = p.v("mm")
                pbuf = p_f[j % 2]
                ndN, ndD = ndp[0], ndp[1]
                for half in range(2):
                    p.add("scalar",
                          (lambda pb_, dst: lambda e: e.activation(dst, pb_[:], AF.Exp))(
                              PB[kb0 + half], pbuf[:, half * 512:(half + 1) * 512]),
                          waits=[("mm", mm_k)] if half == 0 else (), inc=("ac", 1))
                pexp_done[j] = p.v("ac")
                ndw = [("dnd", nd_store[j - 1])] if j >= 1 else []
                for half in range(2):
                    p.add("vector",
                          (lambda pb_, dst, src: lambda e: e.tensor_mul(dst, src, pb_[:]))(
                              PB[vb0 + half], pv_f[:, half * 512:(half + 1) * 512],
                              pbuf[:, half * 512:(half + 1) * 512]),
                          waits=([("mm", mm_v), ("ac", pexp_done[j])] + ndw)
                          if half == 0 else (), inc=("ve", 1))
                pv_done[j] = p.v("ve")
                lamsl = lam_vec[:, j:j + 1]
                p.add("vector",
                      (lambda ls: lambda e: e.tensor_scalar(
                          lamt[:], jramp[:], 0.0, ls, ALU.mult, ALU.add))(lamsl),
                      waits=[("dvl", ldj)], inc=("ve", 1))
                p.add("vector",
                      lambda e: e.tensor_tensor_scan(
                          ndN[:, 1:T + 1], lamt[:], pv_f[:], 0.0, ALU.mult, ALU.add),
                      inc=("ve", 1))
                p.add("vector",
                      (lambda pb2: lambda e: e.tensor_tensor_scan(
                          ndD[:, 1:T + 1], lamt[:], pb2[:], 0.0, ALU.mult, ALU.add))(pbuf),
                      inc=("ve", 1))
                p.add("vector",
                      (lambda jj: lambda e: e.tensor_copy(
                          states[:, jj:jj + 1], ndN[:, T:T + 1]))(j), inc=("ve", 1))
                p.add("vector",
                      (lambda jj: lambda e: e.tensor_copy(
                          states[:, NCH + jj:NCH + jj + 1], ndD[:, T:T + 1]))(j),
                      inc=("ve", 1))
                scans_done[j] = p.v("ve")
                self.dma("sync",
                         (lambda jj: lambda e: e.dma_start(
                             out=self.nN_h[jj * 128:(jj + 1) * 128, :], in_=ndN[:]))(j),
                         "dnd", waits=[("ve", scans_done[j])])
                nd_store[j] = self.dma("sync",
                                       (lambda jj: lambda e: e.dma_start(
                                           out=self.nD_h[jj * 128:(jj + 1) * 128, :], in_=ndD[:]))(j),
                                       "dnd")
                self.emit_prep_tile(2 * j, PB)
                self.emit_prep_tile(2 * j + 1, PB)
            v_allscan = p.v("ve")

            stg = self.dma("gpsimd", lambda e: e.dma_start(out=self.cc0_in[:], in_=states[:]),
                           "dcc", waits=[("ve", v_allscan)])
            p.add("gpsimd", lambda e: e.collective_compute(
                "AllGather", ALU.bypass, replica_groups=[list(range(NCORES))],
                ins=[self.cc0_in[:]], outs=[self.cc0_out[:]]),
                  waits=[("dcc", stg)], inc=("cc", 1), auto=False)
            v_cc0 = p.v("cc")
            p.add("vector", lambda e: e.memset(sN[:], 0.0), inc=("ve", 1))
            p.add("vector", lambda e: e.memset(sD[:], 0.0), inc=("ve", 1))
            for r in range(NCORES):
                ldsh = self.dma("sync",
                                (lambda rr: lambda e: e.dma_start(
                                    out=shard[:], in_=self.cc0_out[rr * 128:(rr + 1) * 128, :]))(r),
                                "dre", waits=[("cc", v_cc0), ("ve", p.v("ve"))])
                ms = self.msel_sb[:, r:r + 1]
                p.add("vector",
                      (lambda m_: lambda e: e.scalar_tensor_tensor(
                          sN[:], shard[:, 0:NCH], m_, sN[:], ALU.mult, ALU.add))(ms),
                      waits=[("dre", ldsh)], inc=("ve", 1))
                p.add("vector",
                      (lambda m_: lambda e: e.scalar_tensor_tensor(
                          sD[:], shard[:, NCH:2 * NCH], m_, sD[:], ALU.mult, ALU.add))(ms),
                      inc=("ve", 1))
            v_sel = p.v("ve")

            def rhs_r(a, half):
                return wkv[a][:, half * 512:(half + 1) * 512]

            r_evac_ac = [0] * NCH

            def evac_r(j, b0, mm_done):
                for half in range(2):
                    p.add("scalar",
                          (lambda pb_, jj, h_: lambda e: e.activation(
                              y_bf[jj][:, h_ * 512:(h_ + 1) * 512], pb_[:],
                              AF.Sigmoid))(PB[b0 + half], j, half),
                          waits=[("mm", mm_done)] if half == 0 else (), inc=("ac", 1))
                r_evac_ac[j] = p.v("ac")
                return (p.v("ve"), r_evac_ac[j])

            self.emit_proj(self.wr_p, NCH, NCH, rhs_r, evac_r, v_xkv)
            mm_after_r = p.v("mm")


            numb, denb = scr_b_l, scr_c_l
            E1 = scr_a_l
            E1v = E1[:, 0:T]
            rat_done = [0] * NCH
            for a in range(NCH):
                ndN, ndD = ndp[0], ndp[1]
                gate = [("ve", v_sel), ("dnd", nd_store[NCH - 1])]
                if a >= 1:
                    gate.append(("ve", rat_done[a - 1]))
                ldn = self.dma("sync",
                               (lambda aa: lambda e: e.dma_start(
                                   out=ndp[0][:], in_=self.nN_h[aa * 128:(aa + 1) * 128, :]))(a),
                               "dre", waits=gate)
                ldd = self.dma("sync",
                               (lambda aa: lambda e: e.dma_start(
                                   out=ndp[1][:], in_=self.nD_h[aa * 128:(aa + 1) * 128, :]))(a),
                               "dre")
                wsl_ = w_vec[:, a:a + 1]
                p.add("scalar",
                      (lambda ws: lambda e: e.activation(E1v, jramp[:], AF.Exp, scale=ws))(wsl_),
                      waits=[("ve", v_sel), ("ac", p.v("ac"))], inc=("ac", 1))
                vE1 = p.v("ac")
                eus = eu_vec[:, a:a + 1]
                euls = eul_vec[:, a:a + 1]
                p.add("vector",
                      (lambda s_: lambda e: e.scalar_tensor_tensor(
                          numb[:], E1v, s_, ndp[0][:, 0:T],
                          ALU.mult, ALU.bypass))(sN[:, a:a + 1]),
                      waits=[("ac", vE1), ("dre", ldn)], inc=("ve", 1))
                p.add("vector",
                      (lambda s_: lambda e: e.scalar_tensor_tensor(
                          numb[:], ndp[0][:, 0:T], s_, numb[:], ALU.mult, ALU.add))(euls),
                      inc=("ve", 1))
                p.add("vector",
                      (lambda s_: lambda e: e.scalar_tensor_tensor(
                          numb[:], ndp[0][:, 1:T + 1], s_, numb[:], ALU.mult, ALU.add))(eus),
                      inc=("ve", 1))
                p.add("vector",
                      (lambda s_: lambda e: e.scalar_tensor_tensor(
                          denb[:], E1v, s_, ndp[1][:, 0:T],
                          ALU.mult, ALU.bypass))(sD[:, a:a + 1]),
                      waits=[("dre", ldd)], inc=("ve", 1))
                p.add("vector",
                      (lambda s_: lambda e: e.scalar_tensor_tensor(
                          denb[:], ndp[1][:, 0:T], s_, denb[:], ALU.mult, ALU.add))(euls),
                      inc=("ve", 1))
                p.add("vector",
                      (lambda s_: lambda e: e.scalar_tensor_tensor(
                          denb[:], ndp[1][:, 1:T + 1], s_, denb[:], ALU.mult, ALU.add))(eus),
                      inc=("ve", 1))
                vnum = p.v("ve")
                p.add("scalar", lambda e: e.activation(denb[:], denb[:], AF.Ln),
                      waits=[("ve", vnum)], inc=("ac", 1))
                p.add("scalar", lambda e: e.activation(denb[:], denb[:], AF.Exp, scale=-1.0),
                      inc=("ac", 1))
                vr = p.v("ac")
                p.add("vector",
                      (lambda aa: lambda e: e.tensor_mul(wkv[aa][:], numb[:], denb[:]))(a),
                      waits=[("ac", vr), ("mm", mm_after_r)], inc=("ve", 1))
                p.add("vector",
                      (lambda aa: lambda e: e.tensor_mul(
                          wkv[aa][:], wkv[aa][:], y_bf[aa][:, 0:T]))(a),
                      waits=[("ac", r_evac_ac[a])], inc=("ve", 1))
                rat_done[a] = p.v("ve")
            v_rwkv = p.v("ve")

            res_st = [0] * NCH

            def rhs_o(a, half):
                return wkv[a][:, half * 512:(half + 1) * 512]

            def evac_o(j, b0, mm_done):
                xb = self.xch[j % 2]
                xsem = "dxa" if j % 2 == 0 else "dxb"
                xrl = self.dma("sync",
                               (lambda jj, b_: lambda e: e.dma_start(
                                   out=b_[:, 0:T],
                                   in_=self.x_in[jj * 128:(jj + 1) * 128, 1:T + 1]))(j, xb),
                               xsem,
                               waits=[("dst", res_st[j - 2])] if j >= 2 else
                               [("ve", p.v("ve"))])
                for half in range(2):
                    p.add("vector",
                          (lambda pb_, b_, h_: lambda e: e.tensor_add(
                              b_[:, h_ * 512:(h_ + 1) * 512],
                              b_[:, h_ * 512:(h_ + 1) * 512], pb_[:]))(
                              PB[b0 + half], xb, half),
                          waits=[("mm", mm_done), (xsem, xrl)] if half == 0 else (),
                          inc=("ve", 1))
                vv = p.v("ve")
                res_st[j] = self.dma("sync",
                                     (lambda jj, b_: lambda e: e.dma_start(
                                         out=self.resid_a[jj * 128:(jj + 1) * 128, :],
                                         in_=b_[:, 0:T]))(j, xb),
                                     "dst", waits=[("ve", vv)])
                return (vv, 0)

            self.emit_proj(self.wo_p, NCH, NCH, rhs_o, evac_o, v_rwkv)
            self.v_l0a_st = p.v("dst")

    def l0f(self):
        nc, p = self.nc, self.p
        VC2 = {"s_gate": 0, "s_up": 44, "s_down": 88}

        with ExitStack() as sc:
            E = lambda n, s, d: self.sb(n, s, d, sc)
            PB = self.PB + [sc.enter_context(nc.psum_tensor("pb7f", [128, 512], F32))]
            self.alloc_scratch(sc)
            pg = ([("pmm", 32)] +
                  [("dwd%d" % k, self.p.v("dwd%d" % k)) for k in range(4)])
            p.add("vector", lambda e: e.nop(), waits=pg)
            p.add("scalar", lambda e: e.nop(), waits=pg)
            y2 = [E(f"y2_{a}", (128, T), BF16) for a in range(NCH)]
            xmid = [E(f"xm{a}", (128, T), BF16) for a in range(NH0)]
            dslab = E("dslab", (128, 2 * HID0), BF16)
            Sy = E("Sy", (128, T), BF16)
            Smid = E("Smid", (128, T), BF16)
            neg128 = E("neg128", (1, 128), BF16)
            v2 = E("v2sb", (128, 128), F32)

            ldv2 = self.dma("sync", lambda e: e.dma_start(out=v2[:], in_=self.vecs2_d[:]),
                            "dvl")
            p.add("vector", lambda e: e.memset(neg128[:], -128.0))
            v_n128 = p.v("ve")

            self.emit_ln(self.resid_a, T, y2)

            guard = [("ve", p.v("ve")), ("ac", p.v("ac"))]
            mm_sy = 0
            for half in range(2):
                for a in range(NCH):
                    mm_sy = p.add("tensor",
                          (lambda pb_, sl_, f_, l_: lambda e: e.matmul(
                              pb_[:], self.ones_bf[:], sl_, start=f_, stop=l_))(
                              PB[half], y2[a][:, half * 512:(half + 1) * 512],
                              a == 0, a == NCH - 1),
                          waits=guard if a == 0 else (),
                          inc=("mm", 1) if (a == NCH - 1 and half == 1) else None) or mm_sy
            p.add("vector", lambda e: e.tensor_copy(Sy[:, 0:512], PB[0][:]),
                  waits=[("mm", mm_sy)])
            p.add("vector", lambda e: e.tensor_copy(Sy[:, 512:1024], PB[1][:]))
            v_sy = p.v("ve")

            gsl = [self.wsl[:, i * C:(i + 1) * C] for i in range(2)]
            usl = [self.wsl[:, (2 + i) * C:(3 + i) * C] for i in range(2)]
            mm0 = p.v("mm")
            slab_user = [mm0, mm0, mm0, mm0]
            xm_done = [0] * NH0
            sg_done = [0] * NH0
            for j in range(NH0):
                gb, ub = gsl[j % 2], usl[j % 2]
                wsem = "dwa" if j % 2 == 0 else "dwb"
                wgd = self.dma("gpsimd",
                               (lambda jj, s_: lambda e: e.dma_start(
                                   out=s_, in_=self.wg_p[jj].rearrange("p a q -> p (a q)")))(j, gb),
                               wsem, waits=[("mm", slab_user[j % 2])])
                wud = self.dma("gpsimd",
                               (lambda jj, s_: lambda e: e.dma_start(
                                   out=s_, in_=self.wu_p[jj].rearrange("p a q -> p (a q)")))(j, ub),
                               wsem, waits=[("mm", slab_user[2 + j % 2])])
                gb0 = 0 if j % 2 == 0 else 4
                ub0 = 2 if j % 2 == 0 else 6
                bw = []
                if j >= 2:
                    bw = [("ac", sg_done[j - 2]), ("ve", xm_done[j - 2])]
                for (pb0, wbuf, wwait, si) in ((gb0, gb, wgd, j % 2),
                                               (ub0, ub, wud, 2 + j % 2)):
                    for half in range(2):
                        pb = PB[pb0 + half]
                        for a in range(NCH):
                            p.add("tensor",
                                  (lambda pb_, l_, r_, f_: lambda e: e.matmul(
                                      pb_[:], l_, r_, start=f_, stop=False))(
                                      pb, wbuf[:, a * 128:(a + 1) * 128],
                                      y2[a][:, half * 512:(half + 1) * 512],
                                      a == 0),
                                  waits=([(wsem, wwait), ("ve", v_sy)] + bw)
                                  if a == 0 else (), inc=None)
                        is_last = (half == 1)
                        r = p.add("tensor",
                              (lambda pb_, h_: lambda e: e.matmul(
                                  pb_[:], neg128[0:1, :],
                                  Sy[0:1, h_ * 512:(h_ + 1) * 512],
                                  start=False, stop=True))(pb, half),
                              waits=[("ve", v_n128)],
                              inc=("mm", 1) if is_last else None)
                        if is_last:
                            slab_user[si] = r
                mm_g = p.v("mm") - 1
                mm_u = p.v("mm")
                sgl = v2[:, VC2["s_gate"] + j:VC2["s_gate"] + j + 1]
                sul = v2[:, VC2["s_up"] + j:VC2["s_up"] + j + 1]
                sbuf_g = self.xch[j % 2]
                for half in range(2):
                    p.add("scalar",
                          (lambda pb_, dst, s_: lambda e: e.activation(
                              dst, pb_[:], AF.Silu, scale=s_))(
                              PB[gb0 + half],
                              sbuf_g[:, half * 512:(half + 1) * 512], sgl),
                          waits=[("mm", mm_g), ("dvl", ldv2)] if half == 0 else ())
                sg_done[j] = p.v("ac")
                for half in range(2):
                    p.add("vector",
                          (lambda pb_, jj, h_, s_, sg_: lambda e: e.scalar_tensor_tensor(
                              xmid[jj][:, h_ * 512:(h_ + 1) * 512], pb_[:], s_,
                              sg_[:, h_ * 512:(h_ + 1) * 512],
                              ALU.mult, ALU.mult))(
                              PB[ub0 + half], j, half, sul, sbuf_g),
                          waits=[("mm", mm_u), ("ac", sg_done[j])] if half == 0 else ())
                xm_done[j] = p.v("ve")
            v_xmid = p.v("ve")

            guard2 = [("ve", v_xmid), ("ac", p.v("ac"))]
            mm_sm = 0
            for half in range(2):
                for a in range(NH0):
                    mm_sm = p.add("tensor",
                          (lambda pb_, sl_, f_, l_: lambda e: e.matmul(
                              pb_[:], self.ones_bf[:], sl_, start=f_, stop=l_))(
                              PB[half], xmid[a][:, half * 512:(half + 1) * 512],
                              a == 0, a == NH0 - 1),
                          waits=guard2 if a == 0 else (),
                          inc=("mm", 1) if (a == NH0 - 1 and half == 1) else None) or mm_sm
            p.add("vector", lambda e: e.tensor_copy(Smid[:, 0:512], PB[0][:]),
                  waits=[("mm", mm_sm)])
            p.add("vector", lambda e: e.tensor_copy(Smid[:, 512:1024], PB[1][:]))
            v_sm = p.v("ve")

            res_st = [0] * NCH

            def rhs_d(a, half):
                return xmid[a][:, half * 512:(half + 1) * 512]

            def extra_d(j, pb, half, is_last):
                return p.add("tensor",
                             (lambda pb_, h_: lambda e: e.matmul(
                                 pb_[:], neg128[0:1, :],
                                 Smid[0:1, h_ * 512:(h_ + 1) * 512],
                                 start=False, stop=True))(pb, half),
                             waits=[("ve", v_sm)],
                             inc=("mm", 1) if is_last else None)

            def evac_d(j, b0, mm_done):
                xb = self.xch[j % 2]
                xsem = "dxa" if j % 2 == 0 else "dxb"
                xrl = self.dma("sync",
                               (lambda jj, b_: lambda e: e.dma_start(
                                   out=b_[:, 0:T],
                                   in_=self.resid_a[jj * 128:(jj + 1) * 128, :]))(j, xb),
                               xsem,
                               waits=[("dst", res_st[j - 2])] if j >= 2 else
                               [("ve", p.v("ve")), ("ac", p.v("ac"))])
                sdl = v2[:, VC2["s_down"] + j:VC2["s_down"] + j + 1]
                for half in range(2):
                    p.add("vector",
                          (lambda pb_, b_, h_, s_: lambda e: e.scalar_tensor_tensor(
                              b_[:, h_ * 512:(h_ + 1) * 512], pb_[:], s_,
                              b_[:, h_ * 512:(h_ + 1) * 512],
                              ALU.mult, ALU.add))(PB[b0 + half], xb, half, sdl),
                          waits=[("mm", mm_done), (xsem, xrl)] if half == 0 else ())
                vv = p.v("ve")
                res_st[j] = self.dma("sync",
                                     (lambda jj, b_: lambda e: e.dma_start(
                                         out=self.resid_b[jj * 128:(jj + 1) * 128, :],
                                         in_=b_[:, 0:T]))(j, xb),
                                     "dst", waits=[("ve", vv)])
                return (vv, p.v("ac"))

            self.emit_proj_arena(self.wd_p, NH0, NCH, rhs_d, evac_d, v_sm,
                                 dslab, extra_k=extra_d)

    def l1a(self):
        nc, p = self.nc, self.p
        PB = self.PB
        NB = 8

        with ExitStack() as sc:
            E = lambda n, s, d: self.sb(n, s, d, sc)
            self.PBT = sc.enter_context(nc.psum_tensor("pbt", [128, 1024], BF16))
            self.alloc_scratch(sc)
            scr_a_l, scr_b_l, scr_c_l = self.scr_a, self.scr_b, self.scr_c
            gt2 = [E(f"gt2_{m}", (128, 512), BF16) for m in range(NB)]
            kt2 = [E(f"kt2_{m}", (128, 512), BF16) for m in range(NB)]
            vt2 = [E(f"vt2_{m}", (128, 512), BF16) for m in range(NB)]
            u64 = E("u64", (128, 128), BF16)
            idn = E("idn", (128, 128), BF16)
            msk = E("msk", (128, 64), BF16)
            v3 = E("v3sb", (128, 16), F32)
            qhb = [E(f"qhb{i}", (128, T), BF16) for i in range(2)]
            ohb = [E(f"ohb{i}", (128, T), BF16) for i in range(2)]
            Sbf = E("sbf_s", (128, 2048), BF16)
            Sent = [E(f"sent{i}", (128, 1024), BF16) for i in range(2)]
            on_sb = [E(f"onsb{i}", (128, 1024), BF16) for i in range(2)]
            Smids = [E(f"smids{i}", (128, 1024), BF16) for i in range(2)]
            fsb = E("fsb_arena", (128, 8192), BF16)
            khfm2 = E("khfm2", (128, 256), BF16)
            Asb2 = [E(f"asb2_{i}", (128, 64), BF16) for i in range(2)]
            Sin = self.wsl[:, 2048:4096]
            Lcum = scr_a_l[:, 0:128]
            Lrun = scr_a_l[:, 128:144]
            Lc0 = scr_a_l[:, 160:288]
            expg2 = [scr_c_l[:, 0:128], scr_c_l[:, 128:256]]
            sadd_f = [scr_c_l[:, 256:384], scr_c_l[:, 384:512]]
            smid = [scr_b_l[:, 512:640], scr_b_l[:, 640:768]]

            ldc = self.dma("sync", lambda e: e.dma_start(out=u64[:], in_=self.u64bd_d[:]), "dvl")
            ldc = self.dma("sync", lambda e: e.dma_start(out=idn[:], in_=self.ident_d[:]), "dvl")
            ldc = self.dma("sync", lambda e: e.dma_start(out=msk[:], in_=self.mask64_d[:]), "dvl")
            ldc = self.dma("sync", lambda e: e.dma_start(out=v3[:], in_=self.vecs3_d[:]), "dvl")
            v_cld = ldc

            qst = [0, 0]
            with ExitStack() as scA:
                y3 = [self.sb(f"y3_{a}", (128, T), BF16, scA) for a in range(NCH)]

                v_y3 = self.emit_ln(self.resid_b, T, y3)

                def rhs_q(a, half):
                    return y3[a][:, half * 512:(half + 1) * 512]

                def evac_q(j, b0, mm_done):
                    qb_ = qhb[j % 2]
                    for half in range(2):
                        p.add("scalar",
                              (lambda pb_, q_, h_: lambda e: e.activation(
                                  q_[:, h_ * 512:(h_ + 1) * 512], pb_[:], AF.Silu))(
                                  PB[b0 + half], qb_, half),
                              waits=[("mm", mm_done)] +
                              ([("dob", qst[j % 2])] if qst[j % 2] else [])
                              if half == 0 else ())
                    va = p.v("ac")
                    qst[j % 2] = self.dma("sync",
                                          (lambda jj, q_: lambda e: e.dma_start(
                                              out=self.qh_h[jj * 128:(jj + 1) * 128, :],
                                              in_=q_[:]))(j, qb_),
                                          "dob", waits=[("ac", va)])
                    return (p.v("ve"), va)

                self.emit_proj(self.wq1_p, NCH, NCH, rhs_q, evac_q, v_y3)

                fslX = [[fsb[:, 0:4096], fsb[:, 4096:8192]],
                        [self.wsl[:, 0:4096], self.wsl[:, 4096:8192]]]
                fld = [[0, 0], [0, 0]]
                mm_q = p.v("mm")
                fmm = [[0, 0], [mm_q, mm_q]]
                fit = [0]
                efree2 = [(p.v("ve"), p.v("ac"))] * 2
                ktv2 = [0, 0]
                hl_last_mm = [p.v("mm"), p.v("mm")]
                fi_ac = [0] * 8
                fi_ve = [0] * 8
                sgp = [scr_b_l[:, 0:256], scr_b_l[:, 256:512]]
                ptmp = [self.x2bf[:, 0:256], self.x2bf[:, 256:512]]
                sadd = [scr_b_l[:, 512:640], scr_b_l[:, 640:768]]

                def load_fi(hp):
                    sl2 = hp % 2
                    for kind, wp, sem in ((0, self.wf_p, "dwa"), (1, self.wi_p, "dwb")):
                        fld[sl2][kind] = self.dma(
                            "gpsimd",
                            (lambda wp_, hp_, k_, s2: lambda e: e.dma_start(
                                out=fslX[s2][k_],
                                in_=wp_[hp_].rearrange("p a q -> p (a q)")))(wp, hp, kind, sl2),
                            sem, waits=[("mm", fmm[sl2][kind])])

                def f_tile(hp, m):
                    it = fit[0]; fit[0] += 1
                    par = it % 2
                    pb = PB[4 if par == 0 else 6]
                    sl = hp % 2
                    mm_d = 0
                    for a in range(NCH):
                        mm_d = p.add("tensor",
                              (lambda pb_, a_, m_: lambda e: e.matmul(
                                  pb_[:, 0:256], y3[a_][:, m_ * 128:(m_ + 1) * 128],
                                  fslX[hp % 2][0][:, a_ * 256:(a_ + 1) * 256],
                                  start=a_ == 0, stop=a_ == NCH - 1))(pb, a, m),
                              waits=([("dwa", fld[hp % 2][0]), ("ve", v_y3),
                                      ("ve", efree2[par][0]),
                                      ("ac", efree2[par][1])]) if a == 0 else (),
                              inc=("mm", 1) if a == NCH - 1 else None) or mm_d
                    fmm[hp % 2][0] = mm_d
                    sg = sgp[par]
                    p.add("scalar",
                          (lambda pb_, sg_: lambda e: e.activation(
                              sg_, pb_[:, 0:256], AF.Sigmoid))(pb, sg),
                          waits=[("mm", mm_d), ("ve", ktv2[par]),
                                 ("mm", hl_last_mm[sl])])
                    vsg = p.v("ac")
                    p.add("scalar",
                          (lambda m_, sl_, sg_: lambda e: e.activation(
                              gt2[m_][:, sl_ * 256:sl_ * 256 + 256],
                              sg_, AF.Ln))(m, sl, sg))
                    p.add("vector",
                          (lambda m_, sl_, sg_: lambda e: e.tensor_scalar(
                              kt2[m_][:, sl_ * 256:sl_ * 256 + 256],
                              sg_, -1.0, 1.0, ALU.mult, ALU.add))(m, sl, sg),
                          waits=[("ac", vsg), ("mm", hl_last_mm[sl])])
                    ktv2[par] = p.v("ve")
                    efree2[par] = (p.v("ve"), p.v("ac"))

                def i_tile(hp, m):
                    it = fit[0]; fit[0] += 1
                    par = it % 2
                    pb = PB[4 if par == 0 else 6]
                    sl = hp % 2
                    mm_d = 0
                    for a in range(NCH):
                        mm_d = p.add("tensor",
                              (lambda pb_, a_, m_: lambda e: e.matmul(
                                  pb_[:, 0:256], y3[a_][:, m_ * 128:(m_ + 1) * 128],
                                  fslX[hp % 2][1][:, a_ * 256:(a_ + 1) * 256],
                                  start=a_ == 0, stop=a_ == NCH - 1))(pb, a, m),
                              waits=([("dwb", fld[hp % 2][1]), ("ve", v_y3),
                                      ("ve", efree2[par][0]),
                                      ("ac", efree2[par][1])]) if a == 0 else (),
                              inc=("mm", 1) if a == NCH - 1 else None) or mm_d
                    fmm[hp % 2][1] = mm_d
                    p.add("vector",
                          (lambda pb_, m_, sl_: lambda e: e.tensor_scalar(
                              vt2[m_][:, sl_ * 256:sl_ * 256 + 256], pb_[:, 0:256],
                              0.08838834764831845, None, ALU.mult, ALU.bypass))(pb, m, sl),
                          waits=[("mm", mm_d), ("mm", hl_last_mm[sl])])
                    efree2[par] = (p.v("ve"), efree2[par][1])

                def prep_tile(hp, m):
                    it = fit[0]; fit[0] += 1
                    par = it % 2
                    pb = PB[4 if par == 0 else 6]
                    sl = hp % 2
                    mm_g = p.add("tensor",
                          (lambda pb_, m_, sl_: lambda e: e.matmul(
                              pb_[:, 0:256], u64[:],
                              gt2[m_][:, sl_ * 256:sl_ * 256 + 256],
                              start=True, stop=True))(pb, m, sl),
                          waits=[("ve", efree2[par][0]), ("ac", efree2[par][1]),
                                 ("dvl", v_cld)],
                          inc=("mm", 1))
                    p.add("scalar",
                          (lambda pb_, par_: lambda e: e.activation(
                              ptmp[par_][:], pb_[:, 0:256], AF.Exp, scale=-1.0))(pb, par),
                          waits=[("mm", mm_g), ("ve", ktv2[par])])
                    vex = p.v("ac")
                    p.add("vector",
                          (lambda m_, sl_, par_: lambda e: e.tensor_mul(
                              kt2[m_][:, sl_ * 256:sl_ * 256 + 256],
                              kt2[m_][:, sl_ * 256:sl_ * 256 + 256],
                              ptmp[par_][:]))(m, sl, par),
                          waits=[("ac", vex)])
                    ktv2[par] = p.v("ve")
                    efree2[par] = (p.v("ve"), vex)

                p.add("vector", lambda e: e.memset(Lrun[:], 1.0))
                v_init = p.v("ve")

                eve, eac = p.v("ve"), p.v("ac")
                gc_free = [eac, eac]
                gc_war = [eve, eve]
                tr_free = [eac]
                khfm_war = [p.v("mm")]
                v_khf = [eac]
                a_free2 = [[eve, eve], [eve, eve]]
                p2v = [v_init, v_init]
                intra_free = [eac, eac]
                o25_free = [eve, eve]
                so_mm = [p.v("mm"), p.v("mm")]
                so_free = [[eve, eve], [eve, eve]]
                sst = [[v_init] * 9 for _ in range(2)]
                v_expg2 = [eac, eac]
                v_qt2 = [eve, eve]
                qld = [0, 0]
                ost = [0, 0]

                def sec_head(hp, Bk):
                    sl = hp % 2
                    for hl in range(2):
                        ss = slice(sl * 256 + hl * 128, sl * 256 + hl * 128 + 128)
                        mm_gc = p.add("tensor",
                              (lambda Bk_, ss_, hl_: lambda e: e.matmul(
                                  PB[hl_][:, 0:128], gt2[Bk_][:, ss_], u64[:],
                                  start=True, stop=True))(Bk, ss, hl),
                              waits=[("ac", max(gc_free[hl], fi_ac[hp])),
                                     ("ve", max(v_init, fi_ve[hp],
                                                a_free2[hl][1], p2v[hl]))],
                              inc=("mm", 1))
                        p.add("scalar",
                              (lambda hl_: lambda e: e.activation(
                                  expg2[hl_][:], PB[hl_][:, 0:128], AF.Exp))(hl),
                              waits=[("mm", mm_gc), ("ve", gc_war[hl])])
                        v_expg2[hl] = p.v("ac")
                        gc_free[hl] = v_expg2[hl]
                    mm_tr = 0
                    for hl in range(2):
                        b0 = sl * 256 + hl * 128
                        prevmm = p.v("mm")
                        mm_tr = p.add("tensor",
                              (lambda Bk_, b_, hl_: lambda e: e.transpose(
                                  self.PBT[:, hl_ * 128:hl_ * 128 + 128],
                                  kt2[Bk_][:, b_:b_ + 128], idn[:]))(Bk, b0, hl),
                              waits=[("ac", tr_free[0])] if hl == 0
                                    else [("mm", prevmm)],
                              inc=("mm", 1)) or mm_tr
                    p.add("scalar",
                          lambda e: e.activation(
                              khfm2[:], self.PBT[:, 0:256], AF.Copy),
                          waits=[("mm", mm_tr), ("mm", khfm_war[0])])
                    v_khf[0] = p.v("ac")
                    tr_free[0] = v_khf[0]
                    for hl in range(2):
                        p.add("vector",
                              (lambda q_, Bk_, hl_: lambda e: e.tensor_mul(
                                  q_[:, Bk_ * 128:(Bk_ + 1) * 128],
                                  q_[:, Bk_ * 128:(Bk_ + 1) * 128],
                                  expg2[hl_][:]))(qhb[hl], Bk, hl),
                              waits=[("ac", v_expg2[hl]), ("dob", qld[hl])])
                        v_qt2[hl] = p.v("ve")
                    for hl in range(2):
                        h = 2 * hp + hl
                        p.add("scalar",
                              (lambda h_, Bk_: lambda e: e.activation(
                                  Lcum[:, h_ * 8 + Bk_:h_ * 8 + Bk_ + 1],
                                  Lrun[:, h_:h_ + 1], AF.Copy))(h, Bk),
                              waits=[("ve", p.v("ve"))])
                        p.add("scalar",
                              (lambda h_, Bk_, hl_: lambda e: e.activation(
                                  Lc0[:, h_ * 8 + Bk_:h_ * 8 + Bk_ + 1],
                                  expg2[hl_][:, 63:64], AF.Copy))(h, Bk, hl))
                        v_lcp = p.v("ac")
                        p.add("vector",
                              (lambda h_, hl_: lambda e: e.tensor_mul(
                                  Lrun[:, h_:h_ + 1], Lrun[:, h_:h_ + 1],
                                  expg2[hl_][:, 127:128]))(h, hl),
                              waits=[("ac", v_lcp)])

                def sec_a(hp, Bk):
                    for hl in range(2):
                        for c in range(2):
                            csl = slice(hl * 128 + c * 64, hl * 128 + c * 64 + 64)
                            tks = slice(Bk * 128 + c * 64, Bk * 128 + c * 64 + 64)
                            prevmm = p.v("mm")
                            mm_a = p.add("tensor",
                                  (lambda q_, c_, csl_, hl_, tks_: lambda e: e.matmul(
                                      PB[hl_][c_ * 64:c_ * 64 + 64, 384:448],
                                      khfm2[:, csl_], q_[:, tks_],
                                      start=True, stop=True,
                                      tile_position=(0, c_ * 64)))(qhb[hl], c, csl, hl, tks),
                                  waits=[("ve", max(v_qt2[hl], a_free2[hl][c])),
                                         ("ac", v_khf[0]), ("mm", prevmm)],
                                  inc=("mm", 1))
                            khfm_war[0] = mm_a
                            p.add("vector",
                                  (lambda c_, hl_: lambda e: e.tensor_mul(
                                      Asb2[hl_][c_ * 64:c_ * 64 + 64, :],
                                      PB[hl_][c_ * 64:c_ * 64 + 64, 384:448],
                                      msk[c_ * 64:c_ * 64 + 64, :]))(c, hl),
                                  waits=[("mm", mm_a)])
                            a_free2[hl][c] = p.v("ve")

                def sec_ov(hp, Bk):
                    sl = hp % 2
                    for hl in range(2):
                        ss = slice(sl * 256 + hl * 128, sl * 256 + hl * 128 + 128)
                        mm_i = 0
                        for c in range(2):
                            mm_i = p.add("tensor",
                                  (lambda Bk_, c_, ss_, hl_: lambda e: e.matmul(
                                      PB[2 + hl_][:, c_ * 64:c_ * 64 + 64],
                                      vt2[Bk_][c_ * 64:c_ * 64 + 64, ss_],
                                      Asb2[hl_][c_ * 64:c_ * 64 + 64, :],
                                      start=c_ == 0, stop=c_ == 1))(Bk, c, ss, hl),
                                  waits=[("ve", a_free2[hl][c])] +
                                        ([("ac", intra_free[hl]),
                                          ("ve", o25_free[hl])] if c == 0 else []),
                                  inc=("mm", 1) if c == 1 else None) or mm_i
                        p.add("scalar",
                              (lambda Bk_, hl_: lambda e: e.activation(
                                  on_sb[hl_][:, Bk_ * 128:(Bk_ + 1) * 128],
                                  PB[2 + hl_][:, 0:128], AF.Copy))(Bk, hl),
                              waits=[("mm", mm_i)])
                        intra_free[hl] = p.v("ac")
                    for c in range(2):
                        for hl in range(2):
                            ss = slice(sl * 256 + hl * 128, sl * 256 + hl * 128 + 128)
                            prevmm = p.v("mm")
                            p.add("tensor",
                                  (lambda Bk_, c_, ss_, hl_: lambda e: e.matmul(
                                      PB[hl_][:, 128 + c_ * 128:256 + c_ * 128],
                                      kt2[Bk_][c_ * 64:c_ * 64 + 64, ss_],
                                      vt2[Bk_][c_ * 64:c_ * 64 + 64, ss_],
                                      start=True, stop=True))(Bk, c, ss, hl),
                                  waits=([("ac", v_expg2[hl]),
                                          ("ve", max(p2v[hl], a_free2[hl][1]))]
                                         if c == 0 and hl == 0 else []) +
                                        [("mm", prevmm)],
                                  inc=("mm", 1))
                    mm_ds0 = p.v("mm")
                    for hl in range(2):
                        h = 2 * hp + hl
                        for c in range(2):
                            src = Sent[hl][:, Bk * 128:(Bk + 1) * 128] if c == 0 \
                                else Smids[hl][:, Bk * 128:(Bk + 1) * 128]
                            dst = Sbf[:, h * 128:(h + 1) * 128] if (Bk == NB - 1 and c == 1) \
                                else (Smids[hl][:, Bk * 128:(Bk + 1) * 128] if c == 0
                                      else Sent[hl][:, (Bk + 1) * 128:(Bk + 2) * 128])
                            p.add("vector",
                                  (lambda sr, c_, hl_: lambda e: e.tensor_add(
                                      sadd_f[hl_], sr,
                                      PB[hl_][:, 128 + c_ * 128:256 + c_ * 128]))(src, c, hl),
                                  waits=[("mm", mm_ds0), ("mm", so_mm[hl])]
                                        if c == 0 else ())
                            p.add("vector",
                                  (lambda ds, c_, hl_: lambda e: e.tensor_scalar(
                                      ds, sadd_f[hl_],
                                      expg2[hl_][:, c_ * 64 + 63:c_ * 64 + 64], None,
                                      ALU.mult, ALU.bypass))(dst, c, hl))
                        p2v[hl] = p.v("ve")
                        sst[hl][Bk + 1] = p2v[hl]
                        gc_war[hl] = p2v[hl]

                def p3(hp):
                    def so_pb(hl, r):
                        return PB[2 + hl][:, 128:256] if r == 0 \
                            else PB[5][:, hl * 128:(hl + 1) * 128]
                    for Bk in range(NB):
                        r = Bk % 2
                        for hl in range(2):
                            for c in range(2):
                                S_ = Sent[hl] if c == 0 else Smids[hl]
                                prevmm = p.v("mm")
                                p.add("tensor",
                                      (lambda q_, Bk_, hl_, c_, r_, Ss: lambda e: e.matmul(
                                          so_pb(hl_, r_)[:, c_ * 64:c_ * 64 + 64],
                                          Ss[:, Bk_ * 128:(Bk_ + 1) * 128],
                                          q_[:, Bk_ * 128 + c_ * 64:Bk_ * 128 + c_ * 64 + 64],
                                          start=True, stop=True))(qhb[hl], Bk, hl, c, r, S_),
                                      waits=([("ve", max(sst[hl][Bk + 1], so_free[hl][r])),
                                              ("ac", intra_free[hl]),
                                              ("mm", prevmm)] if c == 0 else
                                             [("mm", prevmm)]),
                                      inc=("mm", 1))
                        mm_so = p.v("mm")
                        so_mm[0] = so_mm[1] = mm_so
                        for hl in range(2):
                            p.add("vector",
                                  (lambda o_, Bk_, hl_, r_: lambda e: e.tensor_add(
                                      o_[:, Bk_ * 128:(Bk_ + 1) * 128],
                                      on_sb[hl_][:, Bk_ * 128:(Bk_ + 1) * 128],
                                      so_pb(hl_, r_)))(ohb[hl], Bk, hl, r),
                                  waits=[("mm", mm_so), ("dob", ost[hl])]
                                        if Bk <= 1 else [("mm", mm_so)])
                            so_free[hl][r] = p.v("ve")
                    for hl in range(2):
                        o25_free[hl] = p.v("ve")

                load_fi(0)
                load_fi(1)
                for m in range(NB):
                    f_tile(0, m)
                    i_tile(0, m)
                    prep_tile(0, m)
                fi_ac[0] = p.v("ac")
                fi_ve[0] = p.v("ve")

                for hp in range(NHEAD // 2):
                    if hp + 2 < NHEAD // 2:
                        load_fi(hp + 2)
                    for hl0 in range(2):
                        h = 2 * hp + hl0
                        qld[hl0] = self.dma("sync",
                                       (lambda h_, q_: lambda e: e.dma_start(
                                           out=q_[:], in_=self.qh_h[h_ * 128:(h_ + 1) * 128, :]))(h, qhb[hl0]),
                                       "dob")
                    for hl0 in range(2):
                        sst[hl0][0] = p.v("ve")
                        p.add("vector",
                              (lambda hl_: lambda e: e.memset(
                                  Sent[hl_][:, 0:128], 0.0))(hl0),
                              waits=[("mm", so_mm[hl0])])
                        sst[hl0][0] = p.v("ve")
                    for Bk in range(NB):
                        sec_head(hp, Bk)
                        if hp + 1 < NHEAD // 2:
                            f_tile(hp + 1, Bk)
                        sec_a(hp, Bk)
                        sec_ov(hp, Bk)
                        if hp + 1 < NHEAD // 2:
                            i_tile(hp + 1, Bk)
                            prep_tile(hp + 1, Bk)
                    p3(hp)
                    if hp + 1 < NHEAD // 2:
                        fi_ac[hp + 1] = p.v("ac")
                        fi_ve[hp + 1] = p.v("ve")
                    hl_last_mm[hp % 2] = p.v("mm")
                    for hl0 in range(2):
                        h = 2 * hp + hl0
                        self.dma("sync",
                                 (lambda h_, q_: lambda e: e.dma_start(
                                     out=self.qh_h[h_ * 128:(h_ + 1) * 128, :],
                                     in_=q_[:]))(h, qhb[hl0]),
                                 "dob", waits=[("mm", so_mm[hl0])])
                        ost[hl0] = self.dma("sync",
                                      (lambda h_, o_: lambda e: e.dma_start(
                                          out=self.ob_h[h_ * 128:(h_ + 1) * 128, :],
                                          in_=o_[:]))(h, ohb[hl0]),
                                      "dob", waits=[("ve", so_free[hl0][(NB - 1) % 2]),
                                                    ("ac", intra_free[hl0])])
            v_heads_ve, v_heads_ac = p.v("ve"), p.v("ac")

            stg = self.dma("gpsimd", lambda e: e.dma_start(out=self.cc1_in[:], in_=Sbf[:]),
                           "dcc", waits=[("ve", v_heads_ve)])
            p.add("gpsimd", lambda e: e.collective_compute(
                "AllGather", ALU.bypass, replica_groups=[list(range(NCORES))],
                ins=[self.cc1_in[:]], outs=[self.cc1_out[:]]),
                  waits=[("dcc", stg)], inc=("cc", 1), auto=False)
            v_cc1 = p.v("cc")
            p.add("vector", lambda e: e.memset(Sin[:], 0.0),
                  waits=[("mm", p.v("mm"))])
            for r in range(NCORES):
                ldsh = self.dma("sync",
                                (lambda rr: lambda e: e.dma_start(
                                    out=self.xch[0][:, 0:C // 2],
                                    in_=self.cc1_out[rr * 128:(rr + 1) * 128, 0:C // 2]))(r),
                                "dre", waits=[("cc", v_cc1), ("ve", p.v("ve"))])
                ldsh2 = self.dma("sync",
                                 (lambda rr: lambda e: e.dma_start(
                                     out=self.xch[1][:, 0:T],
                                     in_=self.cc1_out[rr * 128:(rr + 1) * 128, C // 2:C]))(r),
                                 "dre")
                ms = self.msel_sb[:, r:r + 1]
                p.add("vector",
                      (lambda m_: lambda e: e.scalar_tensor_tensor(
                          Sin[:, 0:C // 2], self.xch[0][:, 0:C // 2], m_,
                          Sin[:, 0:C // 2], ALU.mult, ALU.add))(ms),
                      waits=[("dre", ldsh)])
                p.add("vector",
                      (lambda m_: lambda e: e.scalar_tensor_tensor(
                          Sin[:, C // 2:C], self.xch[1][:, 0:T], m_,
                          Sin[:, C // 2:C], ALU.mult, ALU.add))(ms),
                      waits=[("dre", ldsh2)])
            v_sin = p.v("ve")

            Scb2 = [self.wsl[:, 4480:4608], self.wsl[:, 4608:4736]]
            x2bfp = [self.x2bf, self.x2bf2]
            hload = [0, 0]
            hfree_ve = [v_sin, v_sin]
            hfree_ac = [0, 0]
            corr_mm = [0, 0]
            mm_stat = 0
            for hp in range(NHEAD // 2):
                for hl0 in range(2):
                    h = 2 * hp + hl0
                    lo = self.dma("sync",
                                  (lambda h_, o_: lambda e: e.dma_start(
                                      out=o_[:], in_=self.ob_h[h_ * 128:(h_ + 1) * 128, :]))(h, ohb[h % 2]),
                                  "dob", waits=[("ve", hfree_ve[h % 2]),
                                                ("ac", hfree_ac[h % 2])])
                    hload[h % 2] = self.dma("sync",
                                  (lambda h_, q_: lambda e: e.dma_start(
                                      out=q_[:], in_=self.qh_h[h_ * 128:(h_ + 1) * 128, :]))(h, qhb[h % 2]),
                                  "dob")
                for Bk in range(NB):
                    for hl in range(2):
                        h = 2 * hp + hl
                        qb_ = qhb[hl]
                        ob_ = ohb[hl]
                        hs = slice(h * 128, (h + 1) * 128)
                        p.add("vector",
                              (lambda h_, Bk_, hs_, hl_: lambda e: e.tensor_scalar(
                                  Scb2[hl_][:], Sin[:, hs_],
                                  Lcum[:, h_ * 8 + Bk_:h_ * 8 + Bk_ + 1], None,
                                  ALU.mult, ALU.bypass))(h, Bk, hs, hl),
                              waits=[("ve", v_sin), ("mm", corr_mm[hl]),
                                     ("dob", hload[hl])])
                        p.add("vector",
                              (lambda q_, h_, Bk_: lambda e: e.tensor_scalar(
                                  q_[:, Bk_ * 128 + 64:Bk_ * 128 + 128],
                                  q_[:, Bk_ * 128 + 64:Bk_ * 128 + 128],
                                  Lc0[:, h_ * 8 + Bk_:h_ * 8 + Bk_ + 1], None,
                                  ALU.mult, ALU.bypass))(qb_, h, Bk))
                        v_sc = p.v("ve")
                        mm_c = p.add("tensor",
                              (lambda q_, Bk_, hl_: lambda e: e.matmul(
                                  PB[hl_][:, 0:128], Scb2[hl_][:],
                                  q_[:, Bk_ * 128:(Bk_ + 1) * 128],
                                  start=True, stop=True))(qb_, Bk, hl),
                              waits=[("ve", v_sc)],
                              inc=("mm", 1))
                        corr_mm[hl] = mm_c
                        p.add("vector",
                              (lambda o_, Bk_, hl_: lambda e: e.tensor_add(
                                  o_[:, Bk_ * 128:(Bk_ + 1) * 128],
                                  o_[:, Bk_ * 128:(Bk_ + 1) * 128],
                                  PB[hl_][:, 0:128]))(ob_, Bk, hl),
                              waits=[("mm", mm_c)])
                for hl0 in range(2):
                    h = 2 * hp + hl0
                    ob_ = ohb[h % 2]
                    p.add("scalar",
                          (lambda o_, hl_: lambda e: e.activation(
                              x2bfp[hl_][:, 0:T], o_[:], AF.Square))(ob_, h % 2),
                          waits=[("ve", p.v("ve"))])
                    v_sq = p.v("ac")
                    for half in range(2):
                        mm_stat = p.add("tensor",
                              (lambda h_, half_, hl_: lambda e: e.matmul(
                                  PB[2 + half_][:], self.ones_bf[:],
                                  x2bfp[hl_][:, half_ * 512:(half_ + 1) * 512],
                                  start=(h_ == 0), stop=(h_ == NHEAD - 1)))(h, half, h % 2),
                              waits=[("ac", v_sq), ("ve", self.v_ones)],
                              inc=("mm", 1) if (h == NHEAD - 1 and half == 1) else None) \
                              or mm_stat
                    hfree_ac[h % 2] = v_sq
                    hfree_ve[h % 2] = p.v("ve")
                    self.dma("sync",
                             (lambda h_, o_: lambda e: e.dma_start(
                                 out=self.ob_h[h_ * 128:(h_ + 1) * 128, :], in_=o_[:]))(h, ob_),
                             "dob", waits=[("ac", v_sq)])

            epssl = self.vec_sb[:, self.VC["eps"]:self.VC["eps"] + 1]
            p.add("vector", lambda e: e.tensor_scalar(
                self.r_tile[:, 0:512], PB[2][:], 1.0 / C, None, ALU.mult, ALU.bypass),
                  waits=[("mm", mm_stat)])
            p.add("vector", lambda e: e.tensor_scalar(
                self.r_tile[:, 512:1024], PB[3][:], 1.0 / C, None, ALU.mult, ALU.bypass))
            v_ms = p.v("ve")
            p.add("scalar", lambda e: e.activation(
                self.r_tile[:, 0:T], self.r_tile[:, 0:T], AF.Ln, bias=epssl),
                  waits=[("ve", v_ms)])
            p.add("scalar", lambda e: e.activation(
                self.r_tile[:, 0:T], self.r_tile[:, 0:T], AF.Exp, scale=-0.5))
            v_rms = p.v("ac")

            on = [E(f"on{a}", (128, T), BF16) for a in range(NCH)]
            ldo = [0, 0]
            for h in range(NCH):
                ob_ = ohb[h % 2]
                ldo[h % 2] = self.dma("sync",
                              (lambda h_, o_: lambda e: e.dma_start(
                                  out=o_[:], in_=self.ob_h[h_ * 128:(h_ + 1) * 128, :]))(h, ob_),
                              "dob",
                              waits=[("ve", p.v("ve")) if h < 2 else ("ve", on_done[h - 2])])
                gsl_ = v3[:, h:h + 1]
                p.add("vector",
                      (lambda h_, o_, g_: lambda e: e.scalar_tensor_tensor(
                          on[h_][:], o_[:], g_, self.r_tile[:, 0:T],
                          ALU.mult, ALU.mult))(h, ob_, gsl_),
                      waits=[("ac", v_rms), ("dob", ldo[h % 2]), ("dvl", v_cld)])
                if h == 0:
                    on_done = [0] * NCH
                on_done[h] = p.v("ve")
            v_on = p.v("ve")

            res_st = [0] * NCH

            def rhs_o1(a, half):
                return on[a][:, half * 512:(half + 1) * 512]

            def evac_o1(j, b0, mm_done):
                xb = self.xch[j % 2]
                xsem = "dxa" if j % 2 == 0 else "dxb"
                xrl = self.dma("sync",
                               (lambda jj, b_: lambda e: e.dma_start(
                                   out=b_[:, 0:T],
                                   in_=self.resid_b[jj * 128:(jj + 1) * 128, :]))(j, xb),
                               xsem,
                               waits=[("dst", res_st[j - 2])] if j >= 2 else
                               [("ve", p.v("ve")), ("ac", p.v("ac"))])
                for half in range(2):
                    p.add("vector",
                          (lambda pb_, b_, h_: lambda e: e.tensor_add(
                              b_[:, h_ * 512:(h_ + 1) * 512],
                              b_[:, h_ * 512:(h_ + 1) * 512], pb_[:]))(
                              PB[b0 + half], xb, half),
                          waits=[("mm", mm_done), (xsem, xrl)] if half == 0 else ())
                vv = p.v("ve")
                res_st[j] = self.dma("sync",
                                     (lambda jj, b_: lambda e: e.dma_start(
                                         out=self.resid_a[jj * 128:(jj + 1) * 128, :],
                                         in_=b_[:, 0:T]))(j, xb),
                                     "dst", waits=[("ve", vv)])
                return (vv, p.v("ac"))

            self.emit_proj(self.wo1_p, NCH, NCH, rhs_o1, evac_o1, v_on)


    def l1f(self):
        nc, p = self.nc, self.p
        PB = self.PB
        V4 = {"ws_up": 0, "ws_gate": 64, "ws_dp": 128, "su_ug": 144,
              "su_dp": 160}
        C_SQ = 2048 ** -0.5
        I_SQ = 8192 ** -0.5

        with ExitStack() as sc:
            E = lambda n, s, d: self.sb(n, s, d, sc)
            pb7 = sc.enter_context(nc.psum_tensor("pb7L", [128, 512], F32))
            PBL = PB + [pb7]
            xt = [E(f"xt{a}", (128, T), BF16) for a in range(NCH)]
            v4 = E("v4sb", (128, 256), F32)
            sup = E("sup", (128, NHI), F32)
            sgt = E("sgt", (128, NHI), F32)
            sdn = E("sdn", (128, NCH), F32)
            h128 = E("h128sb", (128, 128), BF16)

            ldv = self.dma("sync", lambda e: e.dma_start(out=v4[:], in_=self.vecs4_d[:]), "dvl")
            ldh = self.dma("sync", lambda e: e.dma_start(out=h128[:], in_=self.h128_d[:]), "dvl")
            p.add("vector", lambda e: e.tensor_scalar(
                sup[:], v4[:, V4["ws_up"]:V4["ws_up"] + NHI], C_SQ, None,
                ALU.mult, ALU.bypass), waits=[("dvl", ldv)])
            p.add("vector", lambda e: e.tensor_mul(
                sup[:], sup[:], v4[:, V4["su_dp"]:V4["su_dp"] + NHI]))
            p.add("vector", lambda e: e.tensor_scalar(
                sgt[:], v4[:, V4["ws_gate"]:V4["ws_gate"] + NHI], C_SQ, None,
                ALU.mult, ALU.bypass))
            p.add("vector", lambda e: e.tensor_scalar(
                sdn[:], v4[:, V4["ws_dp"]:V4["ws_dp"] + NCH], I_SQ, None,
                ALU.mult, ALU.bypass))
            v_sv = p.v("ve")

            with ExitStack() as scA:
                self.alloc_scratch(scA, big=False)
                y4 = [self.sb(f"y4_{a}", (128, T), BF16, scA) for a in range(NCH)]
                hsu = self.sb("hsu", (128, 2048), BF16, scA)
                hsn = self.sb("hsn", (128, 2048), BF16, scA)

                for a in range(NCH):
                    p.add("vector",
                          (lambda aa: lambda e: e.tensor_scalar(
                              hsu[:, aa * 128:(aa + 1) * 128], h128[:],
                              v4[:, V4["su_ug"] + aa:V4["su_ug"] + aa + 1], None,
                              ALU.mult, ALU.bypass))(a),
                          waits=[("dvl", ldh)])
                    p.add("vector",
                          (lambda aa: lambda e: e.tensor_scalar(
                              hsn[:, aa * 128:(aa + 1) * 128],
                              hsu[:, aa * 128:(aa + 1) * 128], -1.0, None,
                              ALU.mult, ALU.bypass))(a))
                v_hsu = p.v("ve")

                v_y4 = self.emit_ln(self.resid_a, T, y4)

                H16s = hadamard(16)
                efree = [(0, 0)] * 2
                for i in range(NCH):
                    b0 = 0 if i % 2 == 0 else 4
                    mm_a = 0
                    for half in range(2):
                        for j in range(NCH):
                            tile = (hsu if H16s[i, j] > 0 else hsn)
                            mm_a = p.add("tensor",
                                  (lambda pb_, t_, jj, h_, f_, l_: lambda e: e.matmul(
                                      pb_[:], t_[:, jj * 128:(jj + 1) * 128],
                                      y4[jj][:, h_ * 512:(h_ + 1) * 512],
                                      start=f_, stop=l_))(
                                      PBL[b0 + half], tile, j, half,
                                      j == 0, j == NCH - 1),
                                  waits=[("ve", max(v_y4, v_hsu)),
                                         ("ac", efree[i % 2][1])] if j == 0 else (),
                                  inc=("mm", 1) if (half == 1 and j == NCH - 1) else None) or mm_a
                    for half in range(2):
                        p.add("scalar",
                              (lambda pb_, ii, h_: lambda e: e.activation(
                                  xt[ii][:, h_ * 512:(h_ + 1) * 512], pb_[:],
                                  AF.Copy))(PBL[b0 + half], i, half),
                              waits=[("mm", mm_a)] if half == 0 else ())
                    efree[i % 2] = (0, p.v("ac"))
                v_fw1 = p.v("ac")

            xm2 = [E(f"xm2_{a}", (128, T), BF16) for a in range(NHI)]
            gsl = [self.wsl[:, i * C:(i + 1) * C] for i in range(2)]
            usl = [self.wsl[:, (2 + i) * C:(3 + i) * C] for i in range(2)]
            mm0 = p.v("mm")
            slab_user = [mm0] * 4
            xm_done = [0] * NHI
            sg_done = [0] * NHI
            for j in range(NHI):
                gb, ub = gsl[j % 2], usl[j % 2]
                wsem = "dwa" if j % 2 == 0 else "dwb"
                wgd = self.dma("gpsimd",
                               (lambda jj, s_: lambda e: e.dma_start(
                                   out=s_, in_=self.wgt_p[jj].rearrange("p a q -> p (a q)")))(j, gb),
                               wsem, waits=[("mm", slab_user[j % 2])])
                wud = self.dma("gpsimd",
                               (lambda jj, s_: lambda e: e.dma_start(
                                   out=s_, in_=self.wup_p[jj].rearrange("p a q -> p (a q)")))(j, ub),
                               wsem, waits=[("mm", slab_user[2 + j % 2])])
                gb0 = 0 if j % 2 == 0 else 4
                ub0 = 2 if j % 2 == 0 else 6
                bw = []
                if j >= 2:
                    bw = [("ac", sg_done[j - 2]), ("ve", xm_done[j - 2])]
                mmg = mmu = 0
                for (pb0, wbuf, wwait, si) in ((gb0, gb, wgd, j % 2),
                                               (ub0, ub, wud, 2 + j % 2)):
                    for half in range(2):
                        pb = PBL[pb0 + half]
                        for a in range(NCH):
                            r = p.add("tensor",
                                  (lambda pb_, l_, r_, f_, st_: lambda e: e.matmul(
                                      pb_[:], l_, r_, start=f_, stop=st_))(
                                      pb, wbuf[:, a * 128:(a + 1) * 128],
                                      xt[a][:, half * 512:(half + 1) * 512],
                                      a == 0, a == NCH - 1),
                                  waits=([(wsem, wwait), ("ac", v_fw1)] + bw)
                                  if a == 0 else (),
                                  inc=("mm", 1) if (a == NCH - 1 and half == 1) else None)
                            if a == NCH - 1 and half == 1:
                                slab_user[si] = r
                mm_g = p.v("mm") - 1
                mm_u = p.v("mm")
                sgl = sgt[:, j:j + 1]
                sul = sup[:, j:j + 1]
                sbuf_g = self.xch[j % 2]
                for half in range(2):
                    p.add("scalar",
                          (lambda pb_, dst, s_: lambda e: e.activation(
                              dst, pb_[:], AF.Silu, scale=s_))(
                              PBL[gb0 + half],
                              sbuf_g[:, half * 512:(half + 1) * 512], sgl),
                          waits=[("mm", mm_g), ("ve", v_sv)] if half == 0 else ())
                sg_done[j] = p.v("ac")
                for half in range(2):
                    p.add("vector",
                          (lambda pb_, jj, h_, s_, sg_: lambda e: e.scalar_tensor_tensor(
                              xm2[jj][:, h_ * 512:(h_ + 1) * 512], pb_[:], s_,
                              sg_[:, h_ * 512:(h_ + 1) * 512],
                              ALU.mult, ALU.mult))(
                              PBL[ub0 + half], j, half, sul, sbuf_g),
                          waits=[("mm", mm_u), ("ac", sg_done[j])] if half == 0 else ())
                xm_done[j] = p.v("ve")
            v_xm2 = p.v("ve")

            ef2 = [(v_xm2, p.v("ac"))] * 2
            for a in range(NHI):
                b0 = 0 if a % 2 == 0 else 2
                mm_a = 0
                for half in range(2):
                    mm_a = p.add("tensor",
                          (lambda pb_, aa, h_: lambda e: e.matmul(
                              pb_[:], h128[:],
                              xm2[aa][:, h_ * 512:(h_ + 1) * 512],
                              start=True, stop=True))(PBL[b0 + half], a, half),
                          waits=[("ve", v_xm2), ("ac", ef2[a % 2][1]),
                                 ("ve", ef2[a % 2][0])] if half == 0 else (),
                          inc=("mm", 1) if half == 1 else None) or mm_a
                for half in range(2):
                    p.add("scalar",
                          (lambda pb_, aa, h_: lambda e: e.activation(
                              xm2[aa][:, h_ * 512:(h_ + 1) * 512], pb_[:],
                              AF.Copy))(PBL[b0 + half], a, half),
                          waits=[("mm", mm_a)] if half == 0 else ())
                ef2[a % 2] = (p.v("ve"), p.v("ac"))
            v_A2 = p.v("ac")

            v_fw2 = (p.v("ve"), p.v("po"))
            ksl2 = [self.wsl[:, i * 4096:(i + 1) * 4096] for i in range(2)]
            res_st = [0] * NCH
            slab_mm = [p.v("mm")] * 2
            bfree2 = [(p.v("ve"), p.v("ac"))] * 2
            for j in range(NCH):
                b0 = 0 if j % 2 == 0 else 4
                mm_done = 0
                for kh in range(2):
                    sl = ksl2[kh]
                    wsem = "dna" if kh == 0 else "dnb"
                    wd = self.dma("sync",
                                  (lambda jj, kh_, s_: lambda e: e.dma_start(
                                      out=s_,
                                      in_=self.wdn2_h[jj, :, kh_ * 32:(kh_ + 1) * 32, :]
                                      .rearrange("p a q -> p (a q)")))(j, kh, sl),
                                  wsem, waits=[("mm", slab_mm[kh])])
                    for half in range(2):
                        pb = PBL[b0 + half]
                        bw = [("ve", bfree2[j % 2][0]), ("ac", bfree2[j % 2][1])]
                        for a2 in range(32):
                            a = kh * 32 + a2
                            is_last = (kh == 1 and half == 1 and a2 == 31)
                            r = p.add("tensor",
                                  (lambda pb_, l_, r_, f_, st_: lambda e: e.matmul(
                                      pb_[:], l_, r_, start=f_, stop=st_))(
                                      pb, sl[:, a2 * 128:(a2 + 1) * 128],
                                      xm2[a][:, half * 512:(half + 1) * 512],
                                      (kh == 0 and a2 == 0), (kh == 1 and a2 == 31)),
                                  waits=([(wsem, wd), ("ac", v_A2)] + bw)
                                  if a2 == 0 else (),
                                  inc=("mm", 1) if (half == 1 and a2 == 31) else None)
                            if half == 1 and a2 == 31:
                                slab_mm[kh] = r
                                mm_done = r
                xb = self.xch[j % 2]
                xsem = "dxa" if j % 2 == 0 else "dxb"
                xrl = self.dma("sync",
                               (lambda jj, b_: lambda e: e.dma_start(
                                   out=b_[:, 0:T],
                                   in_=self.resid_a[jj * 128:(jj + 1) * 128, :]))(j, xb),
                               xsem,
                               waits=[("dst", res_st[j - 2])] if j >= 2 else
                               [("ve", p.v("ve")), ("ac", p.v("ac"))])
                sdl = sdn[:, j:j + 1]
                for half in range(2):
                    p.add("vector",
                          (lambda pb_, b_, h_, s_: lambda e: e.scalar_tensor_tensor(
                              b_[:, h_ * 512:(h_ + 1) * 512], pb_[:], s_,
                              b_[:, h_ * 512:(h_ + 1) * 512],
                              ALU.mult, ALU.add))(PBL[b0 + half], xb, half, sdl),
                          waits=[("mm", mm_done), (xsem, xrl)] if half == 0 else ())
                vv = p.v("ve")
                bfree2[j % 2] = (vv, p.v("ac"))
                res_st[j] = self.dma("sync",
                                     (lambda jj, b_: lambda e: e.dma_start(
                                         out=self.resid_b[jj * 128:(jj + 1) * 128, :],
                                         in_=b_[:, 0:T]))(j, xb),
                                     "dst", waits=[("ve", vv)])

    def emit_proj_arena(self, wpack, nk, nm, rhs_fn, evac_fn, v_rhs_ready,
                        arena, extra_k=None):
        p = self.p
        PB = self.PB
        kcols = nk * 128
        slabs = [arena[:, i * kcols:(i + 1) * kcols] for i in range(2)]
        mm0 = p.v("mm")
        slab_user = [mm0, mm0]
        bank_free = [(0, 0)] * 2
        wdone = [0] * nm
        for j in range(nm):
            sl = slabs[j % 2]
            wsem = "dwa" if j % 2 == 0 else "dwb"
            wdone[j] = self.dma("gpsimd",
                                (lambda jj, s_: lambda e: e.dma_start(
                                    out=s_, in_=wpack[jj].rearrange("p a q -> p (a q)")))(j, sl),
                                wsem, waits=[("mm", slab_user[j % 2])])
            b0 = 0 if j % 2 == 0 else 4
            bw = []
            if j >= 2:
                bw = [("ve", bank_free[j % 2][0]), ("ac", bank_free[j % 2][1])]
            mm_done = 0
            for half in range(2):
                pb = PB[b0 + half]
                for a in range(nk):
                    is_last = (half == 1 and a == nk - 1 and extra_k is None)
                    mm_done = p.add("tensor",
                          (lambda pb_, l_, r_, f_, st_: lambda e: e.matmul(
                              pb_[:], l_, r_, start=f_, stop=st_))(
                              pb, sl[:, a * 128:(a + 1) * 128], rhs_fn(a, half),
                              a == 0, (a == nk - 1) and extra_k is None),
                          waits=([(wsem, wdone[j]), ("ve", v_rhs_ready)] + bw)
                          if a == 0 else (),
                          inc=("mm", 1) if is_last else None) or mm_done
                if extra_k is not None:
                    mm_done = extra_k(j, pb, half, half == 1) or mm_done
            slab_user[j % 2] = mm_done
            bank_free[j % 2] = evac_fn(j, b0, mm_done)


def prep_inputs(inputs, core, n_sub=4):
    b, h = core // 2, core % 2
    t0 = h * T
    x = inputs["L0_attn_x"]
    xt = np.ascontiguousarray(x[b, t0:t0 + T, :].T)
    if h == 1:
        bcol = np.ascontiguousarray(x[b, t0 - 1, :])[:, None].astype(np.float32)
    else:
        bcol = np.zeros((C, 1), np.float32)
    x_fm = np.ascontiguousarray(np.concatenate([bcol, xt], axis=1))

    vecs = np.zeros((128, 128), np.float32)
    vecs[:, 0:16] = vec_fm(inputs["L0_attn_time_decay"])
    vecs[:, 16:32] = vec_fm(inputs["L0_attn_time_first"])
    vecs[:, 32:48] = vec_fm(inputs["L0_attn_time_mix_k"])
    vecs[:, 48:64] = vec_fm(inputs["L0_attn_time_mix_v"])
    vecs[:, 64:80] = vec_fm(inputs["L0_attn_time_mix_r"])
    vecs[:, 80] = 1e-5

    jramp1 = np.broadcast_to(
        np.arange(0, T, dtype=np.float32)[None, :], (128, T)).copy()
    msel = np.zeros((128, 8), np.float32)
    if h == 1:
        msel[:, core - 1] = 1.0

    def u8(w):
        return np.ascontiguousarray(w).view(np.uint8).reshape(
            w.shape[0], w.shape[1], 4)[..., 0]

    vecs4 = np.zeros((128, 256), np.float32)
    vecs4[:, 0:64] = vec_fm(inputs["L1_ffn_Wscale_ug"][:8192])
    vecs4[:, 64:128] = vec_fm(inputs["L1_ffn_Wscale_ug"][8192:])
    vecs4[:, 128:144] = vec_fm(inputs["L1_ffn_Wscale_dp"])
    vecs4[:, 144:160] = vec_fm(inputs["L1_ffn_SU_ug"])
    vecs4[:, 160:224] = vec_fm(inputs["L1_ffn_SU_dp"])

    vecs2 = np.zeros((128, 128), np.float32)
    vecs2[:, 0:44] = vec_fm(inputs["L0_ffn_s_gate"])
    vecs2[:, 44:88] = vec_fm(inputs["L0_ffn_s_up"])
    vecs2[:, 88:104] = vec_fm(inputs["L0_ffn_s_down"])

    return {
        "x_fm": x_fm,
        "wk_p": pack_lhsT(np.ascontiguousarray(inputs["L0_attn_Wk"].T)),
        "wv_p": pack_lhsT(np.ascontiguousarray(inputs["L0_attn_Wv"].T)),
        "wr_p": pack_lhsT(np.ascontiguousarray(inputs["L0_attn_Wr"].T)),
        "wo_p": pack_lhsT(np.ascontiguousarray(inputs["L0_attn_Wo"].T)),
        "wg_p": pack_lhsT(np.ascontiguousarray(u8(inputs["L0_ffn_wq_gate"]).T)),
        "wu_p": pack_lhsT(np.ascontiguousarray(u8(inputs["L0_ffn_wq_up"]).T)),
        "wd_p": pack_lhsT(np.ascontiguousarray(u8(inputs["L0_ffn_wq_down"]).T)),
        "wup_p": pack_lhsT(np.ascontiguousarray(inputs["L1_ffn_W_up"].T)),
        "wgt_p": pack_lhsT(np.ascontiguousarray(inputs["L1_ffn_W_gate"].T)),
        "wdn_z": np.ascontiguousarray(
            inputs["L1_ffn_W_down"].reshape(2048, 64, 64, 2)
            .transpose(1, 3, 2, 0).reshape(128, 64, 2048)),
        "h64i2": h64i2(),
        "h128": hadamard(128).astype(BF),
        "vecs4": vecs4,
        "wq1_p": pack_lhsT(np.ascontiguousarray(inputs["L1_attn_Wq"])),
        "wo1_p": pack_lhsT(np.ascontiguousarray(inputs["L1_attn_Wo"])),
        "wf_p": pack_rhs(np.ascontiguousarray(inputs["L1_attn_Wf"]), 256),
        "wi_p": pack_rhs(np.ascontiguousarray(inputs["L1_attn_Wi"]), 256),
        "vecs3": vec_fm(inputs["L1_attn_g_weight"]).copy(),
        "ident": np.eye(128, dtype=np.float32).astype(BF),
        "u64bd": u64bd(), "mask64": mask64(),
        "vecs": vecs, "vecs2": vecs2, "jramp1": jramp1, "msel": msel,
    }


def h64i2():
    H = hadamard(64)
    out = np.zeros((128, 128), np.float32)
    for k in range(64):
        for pb in range(2):
            for j in range(64):
                out[k * 2 + pb, pb * 64 + j] = H[k, j]
    return out.astype(BF)


def u64bd():
    u = np.zeros((128, 128), np.float32)
    for i in range(128):
        for j in range(128):
            if i <= j and i // 64 == j // 64:
                u[i, j] = 1.0
    return u.astype(BF)


def mask64():
    m = np.zeros((128, 64), np.float32)
    for i in range(128):
        for j in range(64):
            if (i % 64) <= j:
                m[i, j] = 1.0
    return m.astype(BF)


def kernel(n_sub=4, **inputs):
    inputs = {k: np.asarray(v) for k, v in inputs.items()}
    mk = MK(n_sub=n_sub)
    nc = mk.build()
    in_maps = [prep_inputs(inputs, c, n_sub) for c in range(NCORES)]
    res = run_bass_kernel_spmd(nc, in_maps, core_ids=list(range(NCORES)))
    out = np.zeros((B, 2 * T, C), np.float32)
    for c in range(NCORES):
        b, h = c // 2, c % 2
        out[b, h * T:(h + 1) * T, :] = res.results[c]["out_fm"].T
    return out


# version 10 (final)
# speedup vs baseline: 1.0032x; 1.0032x over previous
import numpy as np
import ml_dtypes
from contextlib import ExitStack
from collections import defaultdict

import concourse.bass as bass
import concourse.mybir as mybir
from concourse.bass_utils import run_bass_kernel_spmd

F32 = mybir.dt.float32
BF16 = mybir.dt.bfloat16
U8 = mybir.dt.uint8
AF = mybir.ActivationFunctionType
ALU = mybir.AluOpType
BF = ml_dtypes.bfloat16

C = 2048
NCH = C // 128
T = 1024
B = 4
HID0 = 5632
NH0 = HID0 // 128
INTER = 8192
NHI = INTER // 128
NHEAD = 16
LS = 64
NCORES = 8

ENGINES = ("tensor", "vector", "scalar", "gpsimd", "sync")


def pack_lhsT(Wl):
    K, M = Wl.shape
    W4 = Wl.reshape(K // 128, 128, M // 128, 128)
    return np.ascontiguousarray(W4.transpose(2, 1, 0, 3))


def pack_rhs(Wr, slab):
    K, N = Wr.shape
    W4 = Wr.reshape(K // 128, 128, N // slab, slab)
    return np.ascontiguousarray(W4.transpose(2, 1, 0, 3))


def vec_fm(v):
    n = v.shape[0]
    return np.ascontiguousarray(v.reshape(n // 128, 128).T)


def hadamard(n):
    H = np.array([[1.0]], dtype=np.float32)
    while H.shape[0] < n:
        H = np.block([[H, H], [H, -H]])
    return H


AUTO_SEM = {"vector": "ve", "scalar": "ac", "gpsimd": "po"}


class Prog:
    def __init__(self):
        self.items = {e: [] for e in ENGINES}
        self.ctr = defaultdict(int)
        self.sem_names = set()

    def add(self, eng, fn, waits=(), inc=None, auto=True):
        if auto and fn is not None and eng in AUTO_SEM:
            s = AUTO_SEM[eng]
            waits = tuple(waits) + ((s, self.ctr[s]),)
            if inc is None:
                inc = (s, 1)
            else:
                assert inc[0] == s, (eng, inc)
        w = tuple((s, v) for (s, v) in waits if v > 0)
        for s, _ in w:
            self.sem_names.add(s)
        r = None
        if inc is not None:
            self.sem_names.add(inc[0])
            self.ctr[inc[0]] += inc[1]
            r = self.ctr[inc[0]]
        self.items[eng].append((fn, w, inc))
        return r

    def v(self, sem):
        return self.ctr[sem]

    def emit(self, eng_name, eng, sems):
        waited = defaultdict(int)
        for fn, waits, inc in self.items[eng_name]:
            for s, val in waits:
                if waited[s] < val:
                    eng.wait_ge(sems[s], val)
                    waited[s] = val
            instr = fn(eng) if fn is not None else None
            if inc is not None:
                instr.then_inc(sems[inc[0]], inc[1])


class MK:
    def __init__(self, n_sub=4):
        self.n_sub = n_sub
        self.nc = bass.Bass("TRN2", target_bir_lowering=False)
        self.p = Prog()
        self.stack = None

    def sb(self, name, shape, dt, stack=None):
        st = stack if stack is not None else self.stack
        return st.enter_context(self.nc.sbuf_tensor(name, list(shape), dt))

    _scr_id = 0

    def alloc_scratch(self, sc, big=True):
        i = MK._scr_id
        MK._scr_id += 1
        self.r_tile = self.sb(f"r_tile{i}", (128, T + 1), F32, sc)
        self.mr_tile = self.sb(f"mr_tile{i}", (128, T + 1), F32, sc)
        self.scr_a = self.sb(f"scr_a{i}", (128, T + 1), F32, sc)
        self.xbf = self.sb(f"xbf{i}", (128, T + 1), BF16, sc)
        self.xbf2 = self.sb(f"xbfb{i}", (128, T + 1), BF16, sc)
        self.x2bf2 = self.sb(f"x2bfb{i}", (128, T + 1), BF16, sc)
        if big:
            self.scr_b = self.sb(f"scr_b{i}", (128, T), F32, sc)
            self.scr_c = self.sb(f"scr_c{i}", (128, T), F32, sc)

    def dma(self, eng, fn, sem, waits=()):
        prev = self.p.v(sem)
        return self.p.add(eng, fn, waits=tuple(waits) + ((sem, prev),),
                          inc=(sem, 16), auto=False)

    def din(self, name, shape, dt=F32):
        return self.nc.dram_tensor(name, list(shape), dt, kind="ExternalInput").ap()

    def dint(self, name, shape, dt=F32):
        return self.nc.dram_tensor(name, list(shape), dt).ap()

    def build(self):
        nc, p = self.nc, self.p
        n_sub = self.n_sub

        self.x_in = self.din("x_fm", (C, T + 1))
        self.wk_p = self.din("wk_p", (NCH, 128, NCH, 128))
        self.wv_p = self.din("wv_p", (NCH, 128, NCH, 128))
        self.wr_p = self.din("wr_p", (NCH, 128, NCH, 128))
        self.wo_p = self.din("wo_p", (NCH, 128, NCH, 128))
        self.vecs = self.din("vecs", (128, 128))
        self.vecs2_d = self.din("vecs2", (128, 128))
        self.wg_p = self.din("wg_p", (NH0, 128, NCH, 128), U8)
        self.wu_p = self.din("wu_p", (NH0, 128, NCH, 128), U8)
        self.wd_p = self.din("wd_p", (NCH, 128, NH0, 128), U8)
        self.jramp1_d = self.din("jramp1", (128, T))
        self.msel_d = self.din("msel", (128, 8))
        self.wq1_p = self.din("wq1_p", (NCH, 128, NCH, 128))
        self.wo1_p = self.din("wo1_p", (NCH, 128, NCH, 128))
        self.wf_p = self.din("wf_p", (8, 128, NCH, 256))
        self.wi_p = self.din("wi_p", (8, 128, NCH, 256))
        self.vecs3_d = self.din("vecs3", (128, 16))
        self.ident_d = self.din("ident", (128, 128), BF16)
        self.u64bd_d = self.din("u64bd", (128, 128), BF16)
        self.mask64_d = self.din("mask64", (128, 64), BF16)
        self.VC = {"td": 0, "tf": 16, "tmk": 32, "tmv": 48, "tmr": 64,
                   "eps": 80, "c_zero": 81}

        self.out_fm = nc.dram_tensor("out_fm", [C, T], F32, kind="ExternalOutput").ap()

        self.resid_a = self.dint("resid_a", (C, T))
        self.resid_b = self.dint("resid_b", (C, T))
        self.nN_h = self.dint("nN_h", (C, T + 1), BF16)
        self.nD_h = self.dint("nD_h", (C, T + 1), BF16)
        self.cc0_in = self.dint("cc0_in", (128, NCH * 2))
        self.cc0_out = self.dint("cc0_out", (NCORES * 128, NCH * 2))
        self.cc1_in = self.dint("cc1_in", (128, C))
        self.cc1_out = self.dint("cc1_out", (NCORES * 128, C))
        self.wdn2_h = self.dint("wdn2_h", (NCH, 128, NHI, 128), BF16)
        self.qh_h = self.dint("qh_h", (C, T), BF16)
        self.ob_h = self.dint("ob_h", (C, T), BF16)
        self.wup_p = self.din("wup_p", (NHI, 128, NCH, 128))
        self.wgt_p = self.din("wgt_p", (NHI, 128, NCH, 128))
        self.wdn_z = self.din("wdn_z", (128, 64, C))
        self.h64i2_d = self.din("h64i2", (128, 128), BF16)
        self.h128_d = self.din("h128", (128, 128), BF16)
        self.vecs4_d = self.din("vecs4", (128, 256))

        with ExitStack() as top:
            self.stack = top
            self.PB = [top.enter_context(self.nc.psum_tensor(f"pb{i}", [128, 512], F32)) for i in range(7)]
            self.vec_sb = self.sb("vec_sb", (128, 128), F32)
            self.ones_bf = self.sb("ones_bf", (128, 128), BF16)
            self.msel_sb = self.sb("msel_sb", (128, 8), F32)
            self.xch = [self.sb(f"xch{i}", (128, T + 1), F32) for i in range(2)]
            self.x2bf = self.sb("x2bf", (128, T + 1), BF16)
            self.wsl = self.sb("wsl_arena", (128, 8192), BF16)

            self.h64i2_sb = self.sb("h64i2_sb", (128, 128), BF16)
            self.prep_dma_i = 0
            self.prep_dma_done = {}
            self.prep_ev = [0, 0]
            self.prep_st = [0, 0, 0, 0]
            self.dma("sync", lambda e: e.dma_start(out=self.vec_sb[:], in_=self.vecs[:]), "dvl")
            self.v_h64 = self.dma("sync", lambda e: e.dma_start(
                out=self.h64i2_sb[:], in_=self.h64i2_d[:]), "dvl")
            nld = self.dma("sync", lambda e: e.dma_start(out=self.msel_sb[:], in_=self.msel_d[:]), "dvl")
            self.v_vecld = nld
            p.add("vector", lambda e: e.memset(self.ones_bf[:], 1.0), inc=("ve", 1))
            self.v_ones = p.v("ve")

            if True:
                self.l0a()
            if n_sub >= 2:
                self.l0f()
            if n_sub >= 3:
                self.l1a()
            if n_sub >= 4:
                self.l1f()

            src = self.resid_a if (n_sub % 2 == 1) else self.resid_b
            bar = self.barrier() + [(sname, p.v(sname)) for sname in list(p.ctr)
                                    if sname.startswith("d")]
            stv = 0
            for a in range(NCH):
                stv = self.dma("sync",
                               (lambda aa, s_: lambda e: e.dma_start(
                                   out=self.out_fm[aa * 128:(aa + 1) * 128, :],
                                   in_=s_[aa * 128:(aa + 1) * 128, :]))(a, src),
                               "dst", waits=bar)
            p.add("sync", None, waits=[("dst", stv)])

            with ExitStack() as ss:
                sems = {s: ss.enter_context(nc.semaphore(s))
                        for s in sorted(p.sem_names)}
                blk = ss.enter_context(nc.Block())
                for name in ENGINES:
                    def mk(nm):
                        def f(eng):
                            p.emit(nm, eng, sems)
                        return f
                    getattr(blk, name)(mk(name))
        return nc

    def emit_prep_dmas(self, n):
        for _ in range(n):
            i = self.prep_dma_i
            if i >= 32:
                return
            self.prep_dma_i += 1
            buf = self.prep_tiles[i % 2]
            self.prep_dma_done[i] = self.dma(
                "gpsimd",
                (lambda ii, b_: lambda e: e.dma_start(
                    out=b_[:], in_=self.wdn_z[:, 2 * ii:2 * ii + 2, :]
                    .rearrange("p a q -> p (a q)")))(i, buf),
                "dwc", waits=[("pmm", max(0, i - 1))])

    def emit_prep_tile(self, i, PB):
        p = self.p
        for m in range(8):
            it = i * 8 + m
            p2 = 2 * i + (m // 4)
            ds = m % 4
            pb = PB[6 + (it % 2)]
            mm = p.add("tensor",
                  (lambda pb_, b_, m_: lambda e: e.matmul(
                      pb_[:], self.h64i2_sb[:],
                      b_[:, m_ * 512:(m_ + 1) * 512],
                      start=True, stop=True))(pb, self.prep_tiles[i % 2], m),
                  waits=[("dwc", self.prep_dma_done[i]),
                         ("ac", self.prep_ev[it % 2]), ("dvl", self.v_h64)],
                  inc=("pmm", 1) if m == 7 else ("mm", 1))
            mmw = ("pmm", p.v("pmm")) if m == 7 else ("mm", mm)
            stg = self.prep_stage[it % 4]
            p.add("scalar",
                  (lambda pb_, st_: lambda e: e.activation(
                      st_[:], pb_[:], AF.Copy))(pb, stg),
                  waits=[mmw, ("dwd%d" % (it % 4), self.prep_st[it % 4])],
                  inc=("ac", 1))
            self.prep_ev[it % 2] = p.v("ac")
            va = p.v("ac")
            jd0 = ds * 4
            self.prep_st[it % 4] = self.dma(
                "sync",
                (lambda st_, p2_, jd0_: lambda e: e.dma_start(
                    out=self.wdn2_h[jd0_:jd0_ + 4, 2 * p2_:2 * p2_ + 2, :, :]
                    .rearrange("a b c d -> b c a d"),
                    in_=st_[:]))(stg, p2, jd0),
                "dwd%d" % (it % 4), waits=[("ac", va)])

    def barrier(self):
        p = self.p
        vals = [("bmm", p.add("tensor", lambda e: e.nop(),
                              waits=[("mm", p.v("mm"))], inc=("bmm", 1))),
                ("ve", p.add("vector", lambda e: e.nop())),
                ("ac", p.add("scalar", lambda e: e.nop())),
                ("po", p.add("gpsimd", lambda e: e.nop()))]
        return vals

    def emit_ln(self, src_hbm, ncols, y_chunks, with_y=True, rms=False,
                src_sb=None):
        nc, p = self.nc, self.p
        PB = self.PB
        xbf_l, r_tile_l, mr_tile_l, scr_a_l = (self.xbf, self.r_tile,
                                               self.mr_tile, self.scr_a)
        xbfp = [xbf_l, self.xbf2]
        x2bfp = [self.x2bf, self.x2bf2]
        nhalf = ncols // 512
        rem = ncols - nhalf * 512
        mm_chunk = [0] * NCH
        ld = [0] * NCH
        entry_ve, entry_ac = p.v("ve"), p.v("ac")
        psum_guard = [("ve", entry_ve), ("ac", entry_ac)]
        for a in range(NCH):
            if src_sb is None:
                buf = self.xch[a % 2]
                sem = "dxa" if a % 2 == 0 else "dxb"
                ld[a] = self.dma("sync",
                                 (lambda aa, b_: lambda e: e.dma_start(
                                     out=b_[:, 0:ncols],
                                     in_=src_hbm[aa * 128:(aa + 1) * 128, :]))(a, buf),
                                 sem, waits=[("ve", p.v("ve")),
                                             ("dst", p.v("dst"))])
            else:
                buf = src_sb[a]
                ld[a] = 0
            gate = [(("dxa" if a % 2 == 0 else "dxb"), ld[a])] if ld[a] else []
            if a >= 2:
                gate.append(("mm", mm_chunk[a - 2]))
            xb_, x2_ = xbfp[a % 2], x2bfp[a % 2]
            p.add("vector",
                  (lambda b_, xb2: lambda e: e.tensor_copy(
                      xb2[:, 0:ncols], b_[:, 0:ncols]))(buf, xb_),
                  waits=gate, inc=("ve", 1))
            vcast = p.v("ve")
            p.add("scalar",
                  (lambda xb2, x22: lambda e: e.activation(
                      x22[:, 0:ncols], xb2[:, 0:ncols], AF.Square))(xb_, x2_),
                  waits=[("ve", vcast)], inc=("ac", 1))
            vsq = p.v("ac")
            first, last = (a == 0), (a == NCH - 1)
            srcs = [(x2_, 2, vsq, "ac", True)]
            if not rms:
                srcs = [(xb_, 0, vcast, "ve", False)] + srcs
            nmm = len(srcs) * (nhalf + (1 if rem else 0))
            imm = 0
            for si, (srt, pb0, wv, ws, is_last_src) in enumerate(srcs):
                is_first_src = (si == 0)
                for hh in range(nhalf):
                    imm += 1
                    mm_chunk[a] = p.add("tensor",
                          (lambda pb_, sl_, f_, l_: lambda e: e.matmul(
                              pb_[:], self.ones_bf[:], sl_, start=f_, stop=l_))(
                              PB[pb0 + hh], srt[:, hh * 512:(hh + 1) * 512], first, last),
                          waits=[(ws, wv), ("ve", self.v_ones)] + psum_guard,
                          inc=("mm", 1) if imm == nmm else None)
                if rem:
                    imm += 1
                    mm_chunk[a] = p.add("tensor",
                          (lambda sl_, f_, l_, co_: lambda e: e.matmul(
                              PB[4][:, co_:co_ + 1], self.ones_bf[:], sl_,
                              start=f_, stop=l_))(
                              srt[:, nhalf * 512:ncols],
                              first and is_first_src, last and is_last_src,
                              pb0 // 2),
                          waits=psum_guard,
                          inc=("mm", 1) if imm == nmm else None)
        mm_stats = p.v("mm")

        def ts(dst, src, s1, s2, op0, op1=ALU.bypass):
            return lambda e: e.tensor_scalar(dst, src, s1, s2, op0, op1)

        vwaits = [("mm", mm_stats)]
        if not rms:
            for hh in range(nhalf):
                p.add("vector", ts(scr_a_l[:, hh * 512:(hh + 1) * 512], PB[hh][:],
                                   1.0 / C, None, ALU.mult), waits=vwaits, inc=("ve", 1))
            if rem:
                p.add("vector", ts(scr_a_l[:, nhalf * 512:ncols], PB[4][:, 0:1],
                                   1.0 / C, None, ALU.mult), inc=("ve", 1))
        for hh in range(nhalf):
            p.add("vector", ts(r_tile_l[:, hh * 512:(hh + 1) * 512], PB[2 + hh][:],
                               1.0 / C, None, ALU.mult), waits=vwaits, inc=("ve", 1))
        if rem:
            p.add("vector", ts(r_tile_l[:, nhalf * 512:ncols], PB[4][:, 1:2],
                               1.0 / C, None, ALU.mult), inc=("ve", 1))
        v_mv = p.v("ve")
        epssl = self.vec_sb[:, self.VC["eps"]:self.VC["eps"] + 1]
        if not rms:
            p.add("scalar",
                  lambda e: e.activation(mr_tile_l[:, 0:ncols],
                                         scr_a_l[:, 0:ncols], AF.Square),
                  waits=[("ve", v_mv)], inc=("ac", 1))
            p.add("vector",
                  lambda e: e.scalar_tensor_tensor(
                      r_tile_l[:, 0:ncols], mr_tile_l[:, 0:ncols], -1.0,
                      r_tile_l[:, 0:ncols], ALU.mult, ALU.add),
                  waits=[("ac", p.v("ac"))], inc=("ve", 1))
        v_var = p.v("ve")
        p.add("scalar",
              lambda e: e.activation(r_tile_l[:, 0:ncols],
                                     r_tile_l[:, 0:ncols], AF.Ln, bias=epssl),
              waits=[("ve", v_var), ("dvl", self.v_vecld)], inc=("ac", 1))
        p.add("scalar",
              lambda e: e.activation(r_tile_l[:, 0:ncols],
                                     r_tile_l[:, 0:ncols], AF.Exp, scale=-0.5),
              inc=("ac", 1))
        v_rstd = p.v("ac")
        if not rms:
            p.add("vector",
                  lambda e: e.tensor_mul(mr_tile_l[:, 0:ncols],
                                         scr_a_l[:, 0:ncols], r_tile_l[:, 0:ncols]),
                  waits=[("ac", v_rstd)], inc=("ve", 1))
        self.v_stats = p.v("ve")
        self.v_stats_ac = v_rstd

        if with_y:
            for a in range(NCH):
                buf = self.xch[a % 2]
                sem = "dxa" if a % 2 == 0 else "dxb"
                if src_sb is None:
                    ld2 = self.dma("sync",
                                   (lambda aa, b_: lambda e: e.dma_start(
                                       out=b_[:, 0:ncols],
                                       in_=src_hbm[aa * 128:(aa + 1) * 128, :]))(a, buf),
                                   sem, waits=[("ve", p.v("ve"))])
                else:
                    buf = src_sb[a]
                    ld2 = 0
                gate = ([(sem, ld2)] if ld2 else []) + [("ve", self.v_stats),
                                                        ("ac", v_rstd)]
                p.add("vector",
                      (lambda b_: lambda e: e.tensor_mul(
                          scr_a_l[:, 0:ncols], b_[:, 0:ncols],
                          r_tile_l[:, 0:ncols]))(buf),
                      waits=gate, inc=("ve", 1))
                p.add("vector",
                      (lambda aa: lambda e: e.tensor_sub(
                          y_chunks[aa][:, 0:ncols], scr_a_l[:, 0:ncols],
                          mr_tile_l[:, 0:ncols]))(a),
                      inc=("ve", 1))
        return p.v("ve")

    def emit_proj(self, wpack, nk, nm, rhs_fn, evac_fn, v_rhs_ready,
                  extra_k=None, wslab_off=0, wdt=None):
        p = self.p
        PB = self.PB
        kcols = nk * 128
        slabs = [self.wsl[:, wslab_off + i * kcols: wslab_off + (i + 1) * kcols]
                 for i in range(2)]
        mm0 = p.v("mm")
        slab_user = [mm0, mm0]
        bank_free = [(0, 0)] * 2
        wdone = [0] * nm
        for j in range(nm):
            sl = slabs[j % 2]
            wsem = "dwa" if j % 2 == 0 else "dwb"
            wdone[j] = self.dma("gpsimd",
                                (lambda jj, s_: lambda e: e.dma_start(
                                    out=s_, in_=wpack[jj].rearrange("p a q -> p (a q)")))(j, sl),
                                wsem, waits=[("mm", slab_user[j % 2])])
            b0 = 0 if j % 2 == 0 else 4
            bw = []
            if j >= 2:
                bw = [("ve", bank_free[j % 2][0]), ("ac", bank_free[j % 2][1])]
            mm_done = 0
            for half in range(2):
                pb = PB[b0 + half]
                for a in range(nk):
                    is_last = (half == 1 and a == nk - 1 and extra_k is None)
                    mm_done = p.add("tensor",
                          (lambda pb_, l_, r_, f_, st_: lambda e: e.matmul(
                              pb_[:], l_, r_, start=f_, stop=st_))(
                              pb, sl[:, a * 128:(a + 1) * 128], rhs_fn(a, half),
                              a == 0, (a == nk - 1) and extra_k is None),
                          waits=([(wsem, wdone[j]), ("ve", v_rhs_ready)] + bw)
                          if a == 0 else (),
                          inc=("mm", 1) if is_last else None) or mm_done
                if extra_k is not None:
                    mm_done = extra_k(j, pb, half, half == 1) or mm_done
            slab_user[j % 2] = mm_done
            bank_free[j % 2] = evac_fn(j, b0, mm_done)

    def l0a(self):
        nc, p = self.nc, self.p
        VC = self.VC

        with ExitStack() as sc:
            E = lambda n, s, d: self.sb(n, s, d, sc)
            PB = self.PB + [sc.enter_context(nc.psum_tensor("pb7a", [128, 512], F32))]
            self.alloc_scratch(sc)
            scr_a_l, scr_b_l, scr_c_l = self.scr_a, self.scr_b, self.scr_c
            y_bf = [E(f"y{a}", (128, T + 1), BF16) for a in range(NCH)]
            gA = [E(f"gA{a}", (128, T), BF16) for a in range(NCH)]
            gB = [E(f"gB{a}", (128, T), BF16) for a in range(NCH)]
            wkv = [E(f"wkv{a}", (128, T), BF16) for a in range(NCH)]
            jramp = E("jramp", (128, T), F32)
            w_vec = E("w_vec", (128, NCH), F32)
            lam_vec = E("lam_vec", (128, NCH), F32)
            eu_vec = E("eu_vec", (128, NCH), F32)
            eul_vec = E("eul_vec", (128, NCH), F32)
            ndp = [E(f"ndp{i}", (128, T + 1), BF16) for i in range(2)]
            p_f = [self.xch[0][:, 0:T], self.xch[1][:, 0:T]]
            pv_f = scr_b_l
            lamt = scr_c_l
            states = E("states", (128, NCH * 2), F32)
            self.prep_tiles = [E(f"prept{i}", (128, 4096), BF16) for i in range(2)]
            self.prep_stage = [self.xbf2[:, 0:512], self.xbf2[:, 512:1024],
                               self.x2bf2[:, 0:512], self.x2bf2[:, 512:1024]]
            shard = E("shard", (128, NCH * 2), F32)
            sN = E("sN", (128, NCH), F32)
            sD = E("sD", (128, NCH), F32)

            ldj = self.dma("sync", lambda e: e.dma_start(out=jramp[:], in_=self.jramp1_d[:]),
                           "dvl")

            tdsl = self.vec_sb[:, VC["td"]:VC["td"] + NCH]
            tfsl = self.vec_sb[:, VC["tf"]:VC["tf"] + NCH]
            p.add("scalar", lambda e: e.activation(w_vec[:], tdsl, AF.Exp),
                  waits=[("dvl", self.v_vecld)], inc=("ac", 1))
            p.add("vector", lambda e: e.tensor_scalar_mul(w_vec[:], w_vec[:], -1.0),
                  waits=[("ac", p.v("ac"))], inc=("ve", 1))
            vw = p.v("ve")
            p.add("scalar", lambda e: e.activation(lam_vec[:], w_vec[:], AF.Exp),
                  waits=[("ve", vw)], inc=("ac", 1))
            p.add("scalar", lambda e: e.activation(eu_vec[:], tfsl, AF.Exp),
                  inc=("ac", 1))
            vac = p.v("ac")
            p.add("vector", lambda e: e.tensor_mul(eul_vec[:], eu_vec[:], lam_vec[:]),
                  waits=[("ac", vac)], inc=("ve", 1))
            p.add("vector", lambda e: e.tensor_scalar(
                eul_vec[:], eul_vec[:], -1.0, 1.0, ALU.mult, ALU.add), inc=("ve", 1))

            self.emit_ln(self.x_in, T + 1, y_bf)

            for a in range(NCH):
                p.add("vector",
                      (lambda aa: lambda e: e.tensor_sub(
                          scr_b_l[:], y_bf[aa][:, 1:T + 1], y_bf[aa][:, 0:T]))(a),
                      inc=("ve", 1))
                for dst, col in ((gA, VC["tmk"]), (gB, VC["tmv"]),
                                 (wkv, VC["tmr"])):
                    sl = self.vec_sb[:, col + a:col + a + 1]
                    p.add("vector",
                          (lambda aa, d_, s_: lambda e: e.scalar_tensor_tensor(
                              d_[aa][:], scr_b_l[:], s_, y_bf[aa][:, 0:T],
                              ALU.mult, ALU.add))(a, dst, sl),
                          inc=("ve", 1))
            v_xkv = p.v("ve")

            p.add("vector", lambda e: e.memset(ndp[0][:, 0:1], 0.0))
            p.add("vector", lambda e: e.memset(ndp[1][:, 0:1], 0.0))

            kcols = C
            ksl = [self.wsl[:, i * kcols:(i + 1) * kcols] for i in range(2)]
            vsl = [self.wsl[:, (2 + i) * kcols:(3 + i) * kcols] for i in range(2)]
            slab_user = [0, 0, 0, 0]
            scans_done = [0] * NCH
            pexp_done = [0] * NCH
            pv_done = [0] * NCH
            nd_store = [0] * NCH
            for j in range(NCH):
                kb, vb = ksl[j % 2], vsl[j % 2]
                wsem = "dwa" if j % 2 == 0 else "dwb"
                wkd = self.dma("gpsimd",
                               (lambda jj, s_: lambda e: e.dma_start(
                                   out=s_, in_=self.wk_p[jj].rearrange("p a q -> p (a q)")))(j, kb),
                               wsem, waits=[("mm", slab_user[j % 2])])
                wvd = self.dma("gpsimd",
                               (lambda jj, s_: lambda e: e.dma_start(
                                   out=s_, in_=self.wv_p[jj].rearrange("p a q -> p (a q)")))(j, vb),
                               wsem, waits=[("mm", slab_user[2 + j % 2])])
                self.emit_prep_dmas(2)
                kb0 = 0 if j % 2 == 0 else 4
                vb0 = 2
                bwk = [("ac", pexp_done[j - 2]), ("ve", scans_done[j - 2])] if j >= 2 else []
                bwv = [("ve", pv_done[j - 1])] if j >= 1 else []
                for (pb0, wbuf, rhs, wwait, si, bw) in (
                        (kb0, kb, gA, wkd, j % 2, bwk),
                        (vb0, vb, gB, wvd, 2 + j % 2, bwv)):
                    for half in range(2):
                        pb = PB[pb0 + half]
                        for a in range(NCH):
                            is_last = (half == 1 and a == NCH - 1)
                            r = p.add("tensor",
                                  (lambda pb_, l_, r_, f_, st_: lambda e: e.matmul(
                                      pb_[:], l_, r_, start=f_, stop=st_))(
                                      pb, wbuf[:, a * 128:(a + 1) * 128],
                                      rhs[a][:, half * 512:(half + 1) * 512],
                                      a == 0, a == NCH - 1),
                                  waits=([(wsem, wwait), ("ve", v_xkv)] + bw)
                                  if a == 0 else (),
                                  inc=("mm", 1) if is_last else None)
                            if is_last:
                                slab_user[si] = r
                mm_k = p.v("mm") - 1
                mm_v = p.v("mm")
                pbuf = p_f[j % 2]
                ndN, ndD = ndp[0], ndp[1]
                for half in range(2):
                    p.add("scalar",
                          (lambda pb_, dst: lambda e: e.activation(dst, pb_[:], AF.Exp))(
                              PB[kb0 + half], pbuf[:, half * 512:(half + 1) * 512]),
                          waits=[("mm", mm_k)] if half == 0 else (), inc=("ac", 1))
                pexp_done[j] = p.v("ac")
                ndw = [("dnd", nd_store[j - 1])] if j >= 1 else []
                for half in range(2):
                    p.add("vector",
                          (lambda pb_, dst, src: lambda e: e.tensor_mul(dst, src, pb_[:]))(
                              PB[vb0 + half], pv_f[:, half * 512:(half + 1) * 512],
                              pbuf[:, half * 512:(half + 1) * 512]),
                          waits=([("mm", mm_v), ("ac", pexp_done[j])] + ndw)
                          if half == 0 else (), inc=("ve", 1))
                pv_done[j] = p.v("ve")
                lamsl = lam_vec[:, j:j + 1]
                p.add("vector",
                      (lambda ls: lambda e: e.tensor_scalar(
                          lamt[:], jramp[:], 0.0, ls, ALU.mult, ALU.add))(lamsl),
                      waits=[("dvl", ldj)], inc=("ve", 1))
                p.add("vector",
                      lambda e: e.tensor_tensor_scan(
                          ndN[:, 1:T + 1], lamt[:], pv_f[:], 0.0, ALU.mult, ALU.add),
                      inc=("ve", 1))
                p.add("vector",
                      (lambda pb2: lambda e: e.tensor_tensor_scan(
                          ndD[:, 1:T + 1], lamt[:], pb2[:], 0.0, ALU.mult, ALU.add))(pbuf),
                      inc=("ve", 1))
                p.add("vector",
                      (lambda jj: lambda e: e.tensor_copy(
                          states[:, jj:jj + 1], ndN[:, T:T + 1]))(j), inc=("ve", 1))
                p.add("vector",
                      (lambda jj: lambda e: e.tensor_copy(
                          states[:, NCH + jj:NCH + jj + 1], ndD[:, T:T + 1]))(j),
                      inc=("ve", 1))
                scans_done[j] = p.v("ve")
                self.dma("sync",
                         (lambda jj: lambda e: e.dma_start(
                             out=self.nN_h[jj * 128:(jj + 1) * 128, :], in_=ndN[:]))(j),
                         "dnd", waits=[("ve", scans_done[j])])
                nd_store[j] = self.dma("sync",
                                       (lambda jj: lambda e: e.dma_start(
                                           out=self.nD_h[jj * 128:(jj + 1) * 128, :], in_=ndD[:]))(j),
                                       "dnd")
                self.emit_prep_tile(2 * j, PB)
                self.emit_prep_tile(2 * j + 1, PB)
            v_allscan = p.v("ve")

            stg = self.dma("gpsimd", lambda e: e.dma_start(out=self.cc0_in[:], in_=states[:]),
                           "dcc", waits=[("ve", v_allscan)])
            p.add("gpsimd", lambda e: e.collective_compute(
                "AllGather", ALU.bypass, replica_groups=[list(range(NCORES))],
                ins=[self.cc0_in[:]], outs=[self.cc0_out[:]]),
                  waits=[("dcc", stg)], inc=("cc", 1), auto=False)
            v_cc0 = p.v("cc")
            p.add("vector", lambda e: e.memset(sN[:], 0.0), inc=("ve", 1))
            p.add("vector", lambda e: e.memset(sD[:], 0.0), inc=("ve", 1))
            for r in range(NCORES):
                ldsh = self.dma("sync",
                                (lambda rr: lambda e: e.dma_start(
                                    out=shard[:], in_=self.cc0_out[rr * 128:(rr + 1) * 128, :]))(r),
                                "dre", waits=[("cc", v_cc0), ("ve", p.v("ve"))])
                ms = self.msel_sb[:, r:r + 1]
                p.add("vector",
                      (lambda m_: lambda e: e.scalar_tensor_tensor(
                          sN[:], shard[:, 0:NCH], m_, sN[:], ALU.mult, ALU.add))(ms),
                      waits=[("dre", ldsh)], inc=("ve", 1))
                p.add("vector",
                      (lambda m_: lambda e: e.scalar_tensor_tensor(
                          sD[:], shard[:, NCH:2 * NCH], m_, sD[:], ALU.mult, ALU.add))(ms),
                      inc=("ve", 1))
            v_sel = p.v("ve")

            def rhs_r(a, half):
                return wkv[a][:, half * 512:(half + 1) * 512]

            r_evac_ac = [0] * NCH

            def evac_r(j, b0, mm_done):
                for half in range(2):
                    p.add("scalar",
                          (lambda pb_, jj, h_: lambda e: e.activation(
                              y_bf[jj][:, h_ * 512:(h_ + 1) * 512], pb_[:],
                              AF.Sigmoid))(PB[b0 + half], j, half),
                          waits=[("mm", mm_done)] if half == 0 else (), inc=("ac", 1))
                r_evac_ac[j] = p.v("ac")
                return (p.v("ve"), r_evac_ac[j])

            self.emit_proj(self.wr_p, NCH, NCH, rhs_r, evac_r, v_xkv)
            mm_after_r = p.v("mm")


            numb, denb = scr_b_l, scr_c_l
            E1 = scr_a_l
            E1v = E1[:, 0:T]
            rat_done = [0] * NCH
            for a in range(NCH):
                ndN, ndD = ndp[0], ndp[1]
                gate = [("ve", v_sel), ("dnd", nd_store[NCH - 1])]
                if a >= 1:
                    gate.append(("ve", rat_done[a - 1]))
                ldn = self.dma("sync",
                               (lambda aa: lambda e: e.dma_start(
                                   out=ndp[0][:], in_=self.nN_h[aa * 128:(aa + 1) * 128, :]))(a),
                               "dre", waits=gate)
                ldd = self.dma("sync",
                               (lambda aa: lambda e: e.dma_start(
                                   out=ndp[1][:], in_=self.nD_h[aa * 128:(aa + 1) * 128, :]))(a),
                               "dre")
                wsl_ = w_vec[:, a:a + 1]
                p.add("scalar",
                      (lambda ws: lambda e: e.activation(E1v, jramp[:], AF.Exp, scale=ws))(wsl_),
                      waits=[("ve", v_sel), ("ac", p.v("ac"))], inc=("ac", 1))
                vE1 = p.v("ac")
                eus = eu_vec[:, a:a + 1]
                euls = eul_vec[:, a:a + 1]
                p.add("vector",
                      (lambda s_: lambda e: e.scalar_tensor_tensor(
                          numb[:], E1v, s_, ndp[0][:, 0:T],
                          ALU.mult, ALU.bypass))(sN[:, a:a + 1]),
                      waits=[("ac", vE1), ("dre", ldn)], inc=("ve", 1))
                p.add("vector",
                      (lambda s_: lambda e: e.scalar_tensor_tensor(
                          numb[:], ndp[0][:, 0:T], s_, numb[:], ALU.mult, ALU.add))(euls),
                      inc=("ve", 1))
                p.add("vector",
                      (lambda s_: lambda e: e.scalar_tensor_tensor(
                          numb[:], ndp[0][:, 1:T + 1], s_, numb[:], ALU.mult, ALU.add))(eus),
                      inc=("ve", 1))
                p.add("vector",
                      (lambda s_: lambda e: e.scalar_tensor_tensor(
                          denb[:], E1v, s_, ndp[1][:, 0:T],
                          ALU.mult, ALU.bypass))(sD[:, a:a + 1]),
                      waits=[("dre", ldd)], inc=("ve", 1))
                p.add("vector",
                      (lambda s_: lambda e: e.scalar_tensor_tensor(
                          denb[:], ndp[1][:, 0:T], s_, denb[:], ALU.mult, ALU.add))(euls),
                      inc=("ve", 1))
                p.add("vector",
                      (lambda s_: lambda e: e.scalar_tensor_tensor(
                          denb[:], ndp[1][:, 1:T + 1], s_, denb[:], ALU.mult, ALU.add))(eus),
                      inc=("ve", 1))
                vnum = p.v("ve")
                p.add("scalar", lambda e: e.activation(denb[:], denb[:], AF.Ln),
                      waits=[("ve", vnum)], inc=("ac", 1))
                p.add("scalar", lambda e: e.activation(denb[:], denb[:], AF.Exp, scale=-1.0),
                      inc=("ac", 1))
                vr = p.v("ac")
                p.add("vector",
                      (lambda aa: lambda e: e.tensor_mul(wkv[aa][:], numb[:], denb[:]))(a),
                      waits=[("ac", vr), ("mm", mm_after_r)], inc=("ve", 1))
                p.add("vector",
                      (lambda aa: lambda e: e.tensor_mul(
                          wkv[aa][:], wkv[aa][:], y_bf[aa][:, 0:T]))(a),
                      waits=[("ac", r_evac_ac[a])], inc=("ve", 1))
                rat_done[a] = p.v("ve")
            v_rwkv = p.v("ve")

            res_st = [0] * NCH

            def rhs_o(a, half):
                return wkv[a][:, half * 512:(half + 1) * 512]

            def evac_o(j, b0, mm_done):
                xb = self.xch[j % 2]
                xsem = "dxa" if j % 2 == 0 else "dxb"
                xrl = self.dma("sync",
                               (lambda jj, b_: lambda e: e.dma_start(
                                   out=b_[:, 0:T],
                                   in_=self.x_in[jj * 128:(jj + 1) * 128, 1:T + 1]))(j, xb),
                               xsem,
                               waits=[("dst", res_st[j - 2])] if j >= 2 else
                               [("ve", p.v("ve"))])
                for half in range(2):
                    p.add("vector",
                          (lambda pb_, b_, h_: lambda e: e.tensor_add(
                              b_[:, h_ * 512:(h_ + 1) * 512],
                              b_[:, h_ * 512:(h_ + 1) * 512], pb_[:]))(
                              PB[b0 + half], xb, half),
                          waits=[("mm", mm_done), (xsem, xrl)] if half == 0 else (),
                          inc=("ve", 1))
                vv = p.v("ve")
                res_st[j] = self.dma("sync",
                                     (lambda jj, b_: lambda e: e.dma_start(
                                         out=self.resid_a[jj * 128:(jj + 1) * 128, :],
                                         in_=b_[:, 0:T]))(j, xb),
                                     "dst", waits=[("ve", vv)])
                return (vv, 0)

            self.emit_proj(self.wo_p, NCH, NCH, rhs_o, evac_o, v_rwkv)
            self.v_l0a_st = p.v("dst")

    def l0f(self):
        nc, p = self.nc, self.p
        VC2 = {"s_gate": 0, "s_up": 44, "s_down": 88}

        with ExitStack() as sc:
            E = lambda n, s, d: self.sb(n, s, d, sc)
            PB = self.PB + [sc.enter_context(nc.psum_tensor("pb7f", [128, 512], F32))]
            self.alloc_scratch(sc)
            pg = ([("pmm", 32)] +
                  [("dwd%d" % k, self.p.v("dwd%d" % k)) for k in range(4)])
            p.add("vector", lambda e: e.nop(), waits=pg)
            p.add("scalar", lambda e: e.nop(), waits=pg)
            y2 = [E(f"y2_{a}", (128, T), BF16) for a in range(NCH)]
            xmid = [E(f"xm{a}", (128, T), BF16) for a in range(NH0)]
            dslab = E("dslab", (128, 2 * HID0), BF16)
            Sy = E("Sy", (128, T), BF16)
            Smid = E("Smid", (128, T), BF16)
            neg128 = E("neg128", (1, 128), BF16)
            v2 = E("v2sb", (128, 128), F32)

            ldv2 = self.dma("sync", lambda e: e.dma_start(out=v2[:], in_=self.vecs2_d[:]),
                            "dvl")
            p.add("vector", lambda e: e.memset(neg128[:], -128.0))
            v_n128 = p.v("ve")

            self.emit_ln(self.resid_a, T, y2)

            guard = [("ve", p.v("ve")), ("ac", p.v("ac"))]
            mm_sy = 0
            for half in range(2):
                for a in range(NCH):
                    mm_sy = p.add("tensor",
                          (lambda pb_, sl_, f_, l_: lambda e: e.matmul(
                              pb_[:], self.ones_bf[:], sl_, start=f_, stop=l_))(
                              PB[half], y2[a][:, half * 512:(half + 1) * 512],
                              a == 0, a == NCH - 1),
                          waits=guard if a == 0 else (),
                          inc=("mm", 1) if (a == NCH - 1 and half == 1) else None) or mm_sy
            p.add("vector", lambda e: e.tensor_copy(Sy[:, 0:512], PB[0][:]),
                  waits=[("mm", mm_sy)])
            p.add("vector", lambda e: e.tensor_copy(Sy[:, 512:1024], PB[1][:]))
            v_sy = p.v("ve")

            gsl = [self.wsl[:, i * C:(i + 1) * C] for i in range(2)]
            usl = [self.wsl[:, (2 + i) * C:(3 + i) * C] for i in range(2)]
            mm0 = p.v("mm")
            slab_user = [mm0, mm0, mm0, mm0]
            xm_done = [0] * NH0
            sg_done = [0] * NH0
            for j in range(NH0):
                gb, ub = gsl[j % 2], usl[j % 2]
                wsem = "dwa" if j % 2 == 0 else "dwb"
                wgd = self.dma("gpsimd",
                               (lambda jj, s_: lambda e: e.dma_start(
                                   out=s_, in_=self.wg_p[jj].rearrange("p a q -> p (a q)")))(j, gb),
                               wsem, waits=[("mm", slab_user[j % 2])])
                wud = self.dma("gpsimd",
                               (lambda jj, s_: lambda e: e.dma_start(
                                   out=s_, in_=self.wu_p[jj].rearrange("p a q -> p (a q)")))(j, ub),
                               wsem, waits=[("mm", slab_user[2 + j % 2])])
                gb0 = 0 if j % 2 == 0 else 4
                ub0 = 2 if j % 2 == 0 else 6
                bw = []
                if j >= 2:
                    bw = [("ac", sg_done[j - 2]), ("ve", xm_done[j - 2])]
                for (pb0, wbuf, wwait, si) in ((gb0, gb, wgd, j % 2),
                                               (ub0, ub, wud, 2 + j % 2)):
                    for half in range(2):
                        pb = PB[pb0 + half]
                        for a in range(NCH):
                            p.add("tensor",
                                  (lambda pb_, l_, r_, f_: lambda e: e.matmul(
                                      pb_[:], l_, r_, start=f_, stop=False))(
                                      pb, wbuf[:, a * 128:(a + 1) * 128],
                                      y2[a][:, half * 512:(half + 1) * 512],
                                      a == 0),
                                  waits=([(wsem, wwait), ("ve", v_sy)] + bw)
                                  if a == 0 else (), inc=None)
                        is_last = (half == 1)
                        r = p.add("tensor",
                              (lambda pb_, h_: lambda e: e.matmul(
                                  pb_[:], neg128[0:1, :],
                                  Sy[0:1, h_ * 512:(h_ + 1) * 512],
                                  start=False, stop=True))(pb, half),
                              waits=[("ve", v_n128)],
                              inc=("mm", 1) if is_last else None)
                        if is_last:
                            slab_user[si] = r
                mm_g = p.v("mm") - 1
                mm_u = p.v("mm")
                sgl = v2[:, VC2["s_gate"] + j:VC2["s_gate"] + j + 1]
                sul = v2[:, VC2["s_up"] + j:VC2["s_up"] + j + 1]
                sbuf_g = self.xch[j % 2]
                for half in range(2):
                    p.add("scalar",
                          (lambda pb_, dst, s_: lambda e: e.activation(
                              dst, pb_[:], AF.Silu, scale=s_))(
                              PB[gb0 + half],
                              sbuf_g[:, half * 512:(half + 1) * 512], sgl),
                          waits=[("mm", mm_g), ("dvl", ldv2)] if half == 0 else ())
                sg_done[j] = p.v("ac")
                for half in range(2):
                    p.add("vector",
                          (lambda pb_, jj, h_, s_, sg_: lambda e: e.scalar_tensor_tensor(
                              xmid[jj][:, h_ * 512:(h_ + 1) * 512], pb_[:], s_,
                              sg_[:, h_ * 512:(h_ + 1) * 512],
                              ALU.mult, ALU.mult))(
                              PB[ub0 + half], j, half, sul, sbuf_g),
                          waits=[("mm", mm_u), ("ac", sg_done[j])] if half == 0 else ())
                xm_done[j] = p.v("ve")
            v_xmid = p.v("ve")

            guard2 = [("ve", v_xmid), ("ac", p.v("ac"))]
            mm_sm = 0
            for half in range(2):
                for a in range(NH0):
                    mm_sm = p.add("tensor",
                          (lambda pb_, sl_, f_, l_: lambda e: e.matmul(
                              pb_[:], self.ones_bf[:], sl_, start=f_, stop=l_))(
                              PB[half], xmid[a][:, half * 512:(half + 1) * 512],
                              a == 0, a == NH0 - 1),
                          waits=guard2 if a == 0 else (),
                          inc=("mm", 1) if (a == NH0 - 1 and half == 1) else None) or mm_sm
            p.add("vector", lambda e: e.tensor_copy(Smid[:, 0:512], PB[0][:]),
                  waits=[("mm", mm_sm)])
            p.add("vector", lambda e: e.tensor_copy(Smid[:, 512:1024], PB[1][:]))
            v_sm = p.v("ve")

            res_st = [0] * NCH

            def rhs_d(a, half):
                return xmid[a][:, half * 512:(half + 1) * 512]

            def extra_d(j, pb, half, is_last):
                return p.add("tensor",
                             (lambda pb_, h_: lambda e: e.matmul(
                                 pb_[:], neg128[0:1, :],
                                 Smid[0:1, h_ * 512:(h_ + 1) * 512],
                                 start=False, stop=True))(pb, half),
                             waits=[("ve", v_sm)],
                             inc=("mm", 1) if is_last else None)

            def evac_d(j, b0, mm_done):
                xb = self.xch[j % 2]
                xsem = "dxa" if j % 2 == 0 else "dxb"
                xrl = self.dma("sync",
                               (lambda jj, b_: lambda e: e.dma_start(
                                   out=b_[:, 0:T],
                                   in_=self.resid_a[jj * 128:(jj + 1) * 128, :]))(j, xb),
                               xsem,
                               waits=[("dst", res_st[j - 2])] if j >= 2 else
                               [("ve", p.v("ve")), ("ac", p.v("ac"))])
                sdl = v2[:, VC2["s_down"] + j:VC2["s_down"] + j + 1]
                for half in range(2):
                    p.add("vector",
                          (lambda pb_, b_, h_, s_: lambda e: e.scalar_tensor_tensor(
                              b_[:, h_ * 512:(h_ + 1) * 512], pb_[:], s_,
                              b_[:, h_ * 512:(h_ + 1) * 512],
                              ALU.mult, ALU.add))(PB[b0 + half], xb, half, sdl),
                          waits=[("mm", mm_done), (xsem, xrl)] if half == 0 else ())
                vv = p.v("ve")
                res_st[j] = self.dma("sync",
                                     (lambda jj, b_: lambda e: e.dma_start(
                                         out=self.resid_b[jj * 128:(jj + 1) * 128, :],
                                         in_=b_[:, 0:T]))(j, xb),
                                     "dst", waits=[("ve", vv)])
                return (vv, p.v("ac"))

            self.emit_proj_arena(self.wd_p, NH0, NCH, rhs_d, evac_d, v_sm,
                                 dslab, extra_k=extra_d)

    def l1a(self):
        nc, p = self.nc, self.p
        PB = self.PB
        NB = 8

        with ExitStack() as sc:
            E = lambda n, s, d: self.sb(n, s, d, sc)
            self.PBT = sc.enter_context(nc.psum_tensor("pbt", [128, 1024], BF16))
            self.alloc_scratch(sc)
            scr_a_l, scr_b_l, scr_c_l = self.scr_a, self.scr_b, self.scr_c
            gt2 = [E(f"gt2_{m}", (128, 512), BF16) for m in range(NB)]
            kt2 = [E(f"kt2_{m}", (128, 512), BF16) for m in range(NB)]
            vt2 = [E(f"vt2_{m}", (128, 512), BF16) for m in range(NB)]
            u64 = E("u64", (128, 128), BF16)
            idn = E("idn", (128, 128), BF16)
            msk = E("msk", (128, 64), BF16)
            v3 = E("v3sb", (128, 16), F32)
            qhb = [E(f"qhb{i}", (128, T), BF16) for i in range(2)]
            ohb = [E(f"ohb{i}", (128, T), BF16) for i in range(2)]
            Sbf = E("sbf_s", (128, 2048), BF16)
            Sent = [E(f"sent{i}", (128, 1024), BF16) for i in range(2)]
            on_sb = [E(f"onsb{i}", (128, 1024), BF16) for i in range(2)]
            Smids = [E(f"smids{i}", (128, 1024), BF16) for i in range(2)]
            fsb = E("fsb_arena", (128, 8192), BF16)
            khfm2 = E("khfm2", (128, 256), BF16)
            Asb2 = [E(f"asb2_{i}", (128, 64), BF16) for i in range(2)]
            Sin = self.wsl[:, 2048:4096]
            Lcum = scr_a_l[:, 0:128]
            Lrun = scr_a_l[:, 128:144]
            Lc0 = scr_a_l[:, 160:288]
            expg2 = [scr_c_l[:, 0:128], scr_c_l[:, 128:256]]
            sadd_f = [scr_c_l[:, 256:384], scr_c_l[:, 384:512]]
            smid = [scr_b_l[:, 512:640], scr_b_l[:, 640:768]]

            ldc = self.dma("sync", lambda e: e.dma_start(out=u64[:], in_=self.u64bd_d[:]), "dvl")
            ldc = self.dma("sync", lambda e: e.dma_start(out=idn[:], in_=self.ident_d[:]), "dvl")
            ldc = self.dma("sync", lambda e: e.dma_start(out=msk[:], in_=self.mask64_d[:]), "dvl")
            ldc = self.dma("sync", lambda e: e.dma_start(out=v3[:], in_=self.vecs3_d[:]), "dvl")
            v_cld = ldc

            qst = [0, 0]
            with ExitStack() as scA:
                y3 = [self.sb(f"y3_{a}", (128, T), BF16, scA) for a in range(NCH)]

                v_y3 = self.emit_ln(self.resid_b, T, y3)

                def rhs_q(a, half):
                    return y3[a][:, half * 512:(half + 1) * 512]

                def evac_q(j, b0, mm_done):
                    qb_ = qhb[j % 2]
                    for half in range(2):
                        p.add("scalar",
                              (lambda pb_, q_, h_: lambda e: e.activation(
                                  q_[:, h_ * 512:(h_ + 1) * 512], pb_[:], AF.Silu))(
                                  PB[b0 + half], qb_, half),
                              waits=[("mm", mm_done)] +
                              ([("dob", qst[j % 2])] if qst[j % 2] else [])
                              if half == 0 else ())
                    va = p.v("ac")
                    qst[j % 2] = self.dma("sync",
                                          (lambda jj, q_: lambda e: e.dma_start(
                                              out=self.qh_h[jj * 128:(jj + 1) * 128, :],
                                              in_=q_[:]))(j, qb_),
                                          "dob", waits=[("ac", va)])
                    return (p.v("ve"), va)

                self.emit_proj(self.wq1_p, NCH, NCH, rhs_q, evac_q, v_y3)

                fslX = [[fsb[:, 0:4096], fsb[:, 4096:8192]],
                        [self.wsl[:, 0:4096], self.wsl[:, 4096:8192]]]
                fld = [[0, 0], [0, 0]]
                mm_q = p.v("mm")
                fmm = [[0, 0], [mm_q, mm_q]]
                fit = [0]
                efree2 = [(p.v("ve"), p.v("ac"))] * 2
                ktv2 = [0, 0]
                hl_last_mm = [p.v("mm"), p.v("mm")]
                fi_ac = [0] * 8
                fi_ve = [0] * 8
                sgp = [scr_b_l[:, 0:256], scr_b_l[:, 256:512]]
                ptmp = [self.x2bf[:, 0:256], self.x2bf[:, 256:512]]
                sadd = [scr_b_l[:, 512:640], scr_b_l[:, 640:768]]

                def load_fi(hp):
                    sl2 = hp % 2
                    for kind, wp, sem in ((0, self.wf_p, "dwa"), (1, self.wi_p, "dwb")):
                        fld[sl2][kind] = self.dma(
                            "gpsimd",
                            (lambda wp_, hp_, k_, s2: lambda e: e.dma_start(
                                out=fslX[s2][k_],
                                in_=wp_[hp_].rearrange("p a q -> p (a q)")))(wp, hp, kind, sl2),
                            sem, waits=[("mm", fmm[sl2][kind])])

                def f_tile(hp, m):
                    it = fit[0]; fit[0] += 1
                    par = it % 2
                    pb = PB[4 if par == 0 else 6]
                    sl = hp % 2
                    mm_d = 0
                    for a in range(NCH):
                        mm_d = p.add("tensor",
                              (lambda pb_, a_, m_: lambda e: e.matmul(
                                  pb_[:, 0:256], y3[a_][:, m_ * 128:(m_ + 1) * 128],
                                  fslX[hp % 2][0][:, a_ * 256:(a_ + 1) * 256],
                                  start=a_ == 0, stop=a_ == NCH - 1))(pb, a, m),
                              waits=([("dwa", fld[hp % 2][0]), ("ve", v_y3),
                                      ("ve", efree2[par][0]),
                                      ("ac", efree2[par][1])]) if a == 0 else (),
                              inc=("mm", 1) if a == NCH - 1 else None) or mm_d
                    fmm[hp % 2][0] = mm_d
                    sg = sgp[par]
                    p.add("scalar",
                          (lambda pb_, sg_: lambda e: e.activation(
                              sg_, pb_[:, 0:256], AF.Sigmoid))(pb, sg),
                          waits=[("mm", mm_d), ("ve", ktv2[par]),
                                 ("mm", hl_last_mm[sl])])
                    vsg = p.v("ac")
                    p.add("scalar",
                          (lambda m_, sl_, sg_: lambda e: e.activation(
                              gt2[m_][:, sl_ * 256:sl_ * 256 + 256],
                              sg_, AF.Ln))(m, sl, sg))
                    p.add("vector",
                          (lambda m_, sl_, sg_: lambda e: e.tensor_scalar(
                              kt2[m_][:, sl_ * 256:sl_ * 256 + 256],
                              sg_, -1.0, 1.0, ALU.mult, ALU.add))(m, sl, sg),
                          waits=[("ac", vsg), ("mm", hl_last_mm[sl])])
                    ktv2[par] = p.v("ve")
                    efree2[par] = (p.v("ve"), p.v("ac"))

                def i_tile(hp, m):
                    it = fit[0]; fit[0] += 1
                    par = it % 2
                    pb = PB[4 if par == 0 else 6]
                    sl = hp % 2
                    mm_d = 0
                    for a in range(NCH):
                        mm_d = p.add("tensor",
                              (lambda pb_, a_, m_: lambda e: e.matmul(
                                  pb_[:, 0:256], y3[a_][:, m_ * 128:(m_ + 1) * 128],
                                  fslX[hp % 2][1][:, a_ * 256:(a_ + 1) * 256],
                                  start=a_ == 0, stop=a_ == NCH - 1))(pb, a, m),
                              waits=([("dwb", fld[hp % 2][1]), ("ve", v_y3),
                                      ("ve", efree2[par][0]),
                                      ("ac", efree2[par][1])]) if a == 0 else (),
                              inc=("mm", 1) if a == NCH - 1 else None) or mm_d
                    fmm[hp % 2][1] = mm_d
                    p.add("vector",
                          (lambda pb_, m_, sl_: lambda e: e.tensor_scalar(
                              vt2[m_][:, sl_ * 256:sl_ * 256 + 256], pb_[:, 0:256],
                              0.08838834764831845, None, ALU.mult, ALU.bypass))(pb, m, sl),
                          waits=[("mm", mm_d), ("mm", hl_last_mm[sl])])
                    efree2[par] = (p.v("ve"), efree2[par][1])

                def prep_tile(hp, m):
                    it = fit[0]; fit[0] += 1
                    par = it % 2
                    pb = PB[4 if par == 0 else 6]
                    sl = hp % 2
                    mm_g = p.add("tensor",
                          (lambda pb_, m_, sl_: lambda e: e.matmul(
                              pb_[:, 0:256], u64[:],
                              gt2[m_][:, sl_ * 256:sl_ * 256 + 256],
                              start=True, stop=True))(pb, m, sl),
                          waits=[("ve", efree2[par][0]), ("ac", efree2[par][1]),
                                 ("dvl", v_cld)],
                          inc=("mm", 1))
                    p.add("scalar",
                          (lambda pb_, par_: lambda e: e.activation(
                              ptmp[par_][:], pb_[:, 0:256], AF.Exp, scale=-1.0))(pb, par),
                          waits=[("mm", mm_g), ("ve", ktv2[par])])
                    vex = p.v("ac")
                    p.add("vector",
                          (lambda m_, sl_, par_: lambda e: e.tensor_mul(
                              kt2[m_][:, sl_ * 256:sl_ * 256 + 256],
                              kt2[m_][:, sl_ * 256:sl_ * 256 + 256],
                              ptmp[par_][:]))(m, sl, par),
                          waits=[("ac", vex)])
                    ktv2[par] = p.v("ve")
                    efree2[par] = (p.v("ve"), vex)

                p.add("vector", lambda e: e.memset(Lrun[:], 1.0))
                v_init = p.v("ve")

                eve, eac = p.v("ve"), p.v("ac")
                gc_free = [eac, eac]
                gc_war = [eve, eve]
                tr_free = [eac]
                khfm_war = [p.v("mm")]
                v_khf = [eac]
                a_free2 = [[eve, eve], [eve, eve]]
                p2v = [v_init, v_init]
                intra_free = [eac, eac]
                o25_free = [eve, eve]
                so_mm = [p.v("mm"), p.v("mm")]
                so_free = [[eve, eve], [eve, eve]]
                sst = [[v_init] * 9 for _ in range(2)]
                v_expg2 = [eac, eac]
                v_qt2 = [eve, eve]
                qld = [0, 0]
                ost = [0, 0]

                def sec_head(hp, Bk):
                    sl = hp % 2
                    for hl in range(2):
                        ss = slice(sl * 256 + hl * 128, sl * 256 + hl * 128 + 128)
                        mm_gc = p.add("tensor",
                              (lambda Bk_, ss_, hl_: lambda e: e.matmul(
                                  PB[hl_][:, 0:128], gt2[Bk_][:, ss_], u64[:],
                                  start=True, stop=True))(Bk, ss, hl),
                              waits=[("ac", max(gc_free[hl], fi_ac[hp])),
                                     ("ve", max(v_init, fi_ve[hp],
                                                a_free2[hl][1], p2v[hl]))],
                              inc=("mm", 1))
                        p.add("scalar",
                              (lambda hl_: lambda e: e.activation(
                                  expg2[hl_][:], PB[hl_][:, 0:128], AF.Exp))(hl),
                              waits=[("mm", mm_gc), ("ve", gc_war[hl])])
                        v_expg2[hl] = p.v("ac")
                        gc_free[hl] = v_expg2[hl]
                    mm_tr = 0
                    for hl in range(2):
                        b0 = sl * 256 + hl * 128
                        prevmm = p.v("mm")
                        mm_tr = p.add("tensor",
                              (lambda Bk_, b_, hl_: lambda e: e.transpose(
                                  self.PBT[:, hl_ * 128:hl_ * 128 + 128],
                                  kt2[Bk_][:, b_:b_ + 128], idn[:]))(Bk, b0, hl),
                              waits=[("ac", tr_free[0])] if hl == 0
                                    else [("mm", prevmm)],
                              inc=("mm", 1)) or mm_tr
                    p.add("scalar",
                          lambda e: e.activation(
                              khfm2[:], self.PBT[:, 0:256], AF.Copy),
                          waits=[("mm", mm_tr), ("mm", khfm_war[0])])
                    v_khf[0] = p.v("ac")
                    tr_free[0] = v_khf[0]
                    for hl in range(2):
                        p.add("vector",
                              (lambda q_, Bk_, hl_: lambda e: e.tensor_mul(
                                  q_[:, Bk_ * 128:(Bk_ + 1) * 128],
                                  q_[:, Bk_ * 128:(Bk_ + 1) * 128],
                                  expg2[hl_][:]))(qhb[hl], Bk, hl),
                              waits=[("ac", v_expg2[hl]), ("dob", qld[hl])])
                        v_qt2[hl] = p.v("ve")
                    for hl in range(2):
                        h = 2 * hp + hl
                        p.add("scalar",
                              (lambda h_, Bk_: lambda e: e.activation(
                                  Lcum[:, h_ * 8 + Bk_:h_ * 8 + Bk_ + 1],
                                  Lrun[:, h_:h_ + 1], AF.Copy))(h, Bk),
                              waits=[("ve", p.v("ve"))])
                        p.add("scalar",
                              (lambda h_, Bk_, hl_: lambda e: e.activation(
                                  Lc0[:, h_ * 8 + Bk_:h_ * 8 + Bk_ + 1],
                                  expg2[hl_][:, 63:64], AF.Copy))(h, Bk, hl))
                        v_lcp = p.v("ac")
                        p.add("vector",
                              (lambda h_, hl_: lambda e: e.tensor_mul(
                                  Lrun[:, h_:h_ + 1], Lrun[:, h_:h_ + 1],
                                  expg2[hl_][:, 127:128]))(h, hl),
                              waits=[("ac", v_lcp)])

                def sec_a(hp, Bk):
                    for hl in range(2):
                        for c in range(2):
                            csl = slice(hl * 128 + c * 64, hl * 128 + c * 64 + 64)
                            tks = slice(Bk * 128 + c * 64, Bk * 128 + c * 64 + 64)
                            prevmm = p.v("mm")
                            mm_a = p.add("tensor",
                                  (lambda q_, c_, csl_, hl_, tks_: lambda e: e.matmul(
                                      PB[hl_][c_ * 64:c_ * 64 + 64, 384:448],
                                      khfm2[:, csl_], q_[:, tks_],
                                      start=True, stop=True,
                                      tile_position=(0, c_ * 64)))(qhb[hl], c, csl, hl, tks),
                                  waits=[("ve", max(v_qt2[hl], a_free2[hl][c])),
                                         ("ac", v_khf[0]), ("mm", prevmm)],
                                  inc=("mm", 1))
                            khfm_war[0] = mm_a
                            p.add("vector",
                                  (lambda c_, hl_: lambda e: e.tensor_mul(
                                      Asb2[hl_][c_ * 64:c_ * 64 + 64, :],
                                      PB[hl_][c_ * 64:c_ * 64 + 64, 384:448],
                                      msk[c_ * 64:c_ * 64 + 64, :]))(c, hl),
                                  waits=[("mm", mm_a)])
                            a_free2[hl][c] = p.v("ve")

                def sec_ov(hp, Bk):
                    sl = hp % 2
                    for hl in range(2):
                        ss = slice(sl * 256 + hl * 128, sl * 256 + hl * 128 + 128)
                        mm_i = 0
                        for c in range(2):
                            mm_i = p.add("tensor",
                                  (lambda Bk_, c_, ss_, hl_: lambda e: e.matmul(
                                      PB[2 + hl_][:, c_ * 64:c_ * 64 + 64],
                                      vt2[Bk_][c_ * 64:c_ * 64 + 64, ss_],
                                      Asb2[hl_][c_ * 64:c_ * 64 + 64, :],
                                      start=c_ == 0, stop=c_ == 1))(Bk, c, ss, hl),
                                  waits=[("ve", a_free2[hl][c])] +
                                        ([("ac", intra_free[hl]),
                                          ("ve", o25_free[hl])] if c == 0 else []),
                                  inc=("mm", 1) if c == 1 else None) or mm_i
                        p.add("scalar",
                              (lambda Bk_, hl_: lambda e: e.activation(
                                  on_sb[hl_][:, Bk_ * 128:(Bk_ + 1) * 128],
                                  PB[2 + hl_][:, 0:128], AF.Copy))(Bk, hl),
                              waits=[("mm", mm_i)])
                        intra_free[hl] = p.v("ac")
                    for c in range(2):
                        for hl in range(2):
                            ss = slice(sl * 256 + hl * 128, sl * 256 + hl * 128 + 128)
                            prevmm = p.v("mm")
                            p.add("tensor",
                                  (lambda Bk_, c_, ss_, hl_: lambda e: e.matmul(
                                      PB[hl_][:, 128 + c_ * 128:256 + c_ * 128],
                                      kt2[Bk_][c_ * 64:c_ * 64 + 64, ss_],
                                      vt2[Bk_][c_ * 64:c_ * 64 + 64, ss_],
                                      start=True, stop=True))(Bk, c, ss, hl),
                                  waits=([("ac", v_expg2[hl]),
                                          ("ve", max(p2v[hl], a_free2[hl][1]))]
                                         if c == 0 and hl == 0 else []) +
                                        [("mm", prevmm)],
                                  inc=("mm", 1))
                    mm_ds0 = p.v("mm")
                    for hl in range(2):
                        h = 2 * hp + hl
                        for c in range(2):
                            src = Sent[hl][:, Bk * 128:(Bk + 1) * 128] if c == 0 \
                                else Smids[hl][:, Bk * 128:(Bk + 1) * 128]
                            dst = Sbf[:, h * 128:(h + 1) * 128] if (Bk == NB - 1 and c == 1) \
                                else (Smids[hl][:, Bk * 128:(Bk + 1) * 128] if c == 0
                                      else Sent[hl][:, (Bk + 1) * 128:(Bk + 2) * 128])
                            p.add("vector",
                                  (lambda sr, c_, hl_: lambda e: e.tensor_add(
                                      sadd_f[hl_], sr,
                                      PB[hl_][:, 128 + c_ * 128:256 + c_ * 128]))(src, c, hl),
                                  waits=[("mm", mm_ds0), ("mm", so_mm[hl])]
                                        if c == 0 else ())
                            p.add("vector",
                                  (lambda ds, c_, hl_: lambda e: e.tensor_scalar(
                                      ds, sadd_f[hl_],
                                      expg2[hl_][:, c_ * 64 + 63:c_ * 64 + 64], None,
                                      ALU.mult, ALU.bypass))(dst, c, hl))
                        p2v[hl] = p.v("ve")
                        sst[hl][Bk + 1] = p2v[hl]
                        gc_war[hl] = p2v[hl]

                def p3(hp):
                    def so_pb(hl, r):
                        return PB[2 + hl][:, 128:256] if r == 0 \
                            else PB[5][:, hl * 128:(hl + 1) * 128]
                    for Bk in range(NB):
                        r = Bk % 2
                        for hl in range(2):
                            for c in range(2):
                                S_ = Sent[hl] if c == 0 else Smids[hl]
                                prevmm = p.v("mm")
                                p.add("tensor",
                                      (lambda q_, Bk_, hl_, c_, r_, Ss: lambda e: e.matmul(
                                          so_pb(hl_, r_)[:, c_ * 64:c_ * 64 + 64],
                                          Ss[:, Bk_ * 128:(Bk_ + 1) * 128],
                                          q_[:, Bk_ * 128 + c_ * 64:Bk_ * 128 + c_ * 64 + 64],
                                          start=True, stop=True))(qhb[hl], Bk, hl, c, r, S_),
                                      waits=([("ve", max(sst[hl][Bk + 1], so_free[hl][r])),
                                              ("ac", intra_free[hl]),
                                              ("mm", prevmm)] if c == 0 else
                                             [("mm", prevmm)]),
                                      inc=("mm", 1))
                        mm_so = p.v("mm")
                        so_mm[0] = so_mm[1] = mm_so
                        for hl in range(2):
                            p.add("vector",
                                  (lambda o_, Bk_, hl_, r_: lambda e: e.tensor_add(
                                      o_[:, Bk_ * 128:(Bk_ + 1) * 128],
                                      on_sb[hl_][:, Bk_ * 128:(Bk_ + 1) * 128],
                                      so_pb(hl_, r_)))(ohb[hl], Bk, hl, r),
                                  waits=[("mm", mm_so), ("dob", ost[hl])]
                                        if Bk <= 1 else [("mm", mm_so)])
                            so_free[hl][r] = p.v("ve")
                    for hl in range(2):
                        o25_free[hl] = p.v("ve")

                load_fi(0)
                load_fi(1)
                for m in range(NB):
                    f_tile(0, m)
                    i_tile(0, m)
                    prep_tile(0, m)
                fi_ac[0] = p.v("ac")
                fi_ve[0] = p.v("ve")

                for hp in range(NHEAD // 2):
                    if hp + 2 < NHEAD // 2:
                        load_fi(hp + 2)
                    for hl0 in range(2):
                        h = 2 * hp + hl0
                        qld[hl0] = self.dma("sync",
                                       (lambda h_, q_: lambda e: e.dma_start(
                                           out=q_[:], in_=self.qh_h[h_ * 128:(h_ + 1) * 128, :]))(h, qhb[hl0]),
                                       "dob")
                    for hl0 in range(2):
                        sst[hl0][0] = p.v("ve")
                        p.add("vector",
                              (lambda hl_: lambda e: e.memset(
                                  Sent[hl_][:, 0:128], 0.0))(hl0),
                              waits=[("mm", so_mm[hl0])])
                        sst[hl0][0] = p.v("ve")
                    for Bk in range(NB):
                        sec_head(hp, Bk)
                        if hp + 1 < NHEAD // 2:
                            f_tile(hp + 1, Bk)
                        sec_a(hp, Bk)
                        sec_ov(hp, Bk)
                        if hp + 1 < NHEAD // 2:
                            i_tile(hp + 1, Bk)
                            prep_tile(hp + 1, Bk)
                    p3(hp)
                    if hp + 1 < NHEAD // 2:
                        fi_ac[hp + 1] = p.v("ac")
                        fi_ve[hp + 1] = p.v("ve")
                    hl_last_mm[hp % 2] = p.v("mm")
                    for hl0 in range(2):
                        h = 2 * hp + hl0
                        self.dma("sync",
                                 (lambda h_, q_: lambda e: e.dma_start(
                                     out=self.qh_h[h_ * 128:(h_ + 1) * 128, :],
                                     in_=q_[:]))(h, qhb[hl0]),
                                 "dob", waits=[("mm", so_mm[hl0])])
                        ost[hl0] = self.dma("sync",
                                      (lambda h_, o_: lambda e: e.dma_start(
                                          out=self.ob_h[h_ * 128:(h_ + 1) * 128, :],
                                          in_=o_[:]))(h, ohb[hl0]),
                                      "dob", waits=[("ve", so_free[hl0][(NB - 1) % 2]),
                                                    ("ac", intra_free[hl0])])
            v_heads_ve, v_heads_ac = p.v("ve"), p.v("ac")

            stg = self.dma("gpsimd", lambda e: e.dma_start(out=self.cc1_in[:], in_=Sbf[:]),
                           "dcc", waits=[("ve", v_heads_ve)])
            p.add("gpsimd", lambda e: e.collective_compute(
                "AllGather", ALU.bypass, replica_groups=[list(range(NCORES))],
                ins=[self.cc1_in[:]], outs=[self.cc1_out[:]]),
                  waits=[("dcc", stg)], inc=("cc", 1), auto=False)
            v_cc1 = p.v("cc")
            p.add("vector", lambda e: e.memset(Sin[:], 0.0),
                  waits=[("mm", p.v("mm"))])
            for r in range(NCORES):
                ldsh = self.dma("sync",
                                (lambda rr: lambda e: e.dma_start(
                                    out=self.xch[0][:, 0:C // 2],
                                    in_=self.cc1_out[rr * 128:(rr + 1) * 128, 0:C // 2]))(r),
                                "dre", waits=[("cc", v_cc1), ("ve", p.v("ve"))])
                ldsh2 = self.dma("sync",
                                 (lambda rr: lambda e: e.dma_start(
                                     out=self.xch[1][:, 0:T],
                                     in_=self.cc1_out[rr * 128:(rr + 1) * 128, C // 2:C]))(r),
                                 "dre")
                ms = self.msel_sb[:, r:r + 1]
                p.add("vector",
                      (lambda m_: lambda e: e.scalar_tensor_tensor(
                          Sin[:, 0:C // 2], self.xch[0][:, 0:C // 2], m_,
                          Sin[:, 0:C // 2], ALU.mult, ALU.add))(ms),
                      waits=[("dre", ldsh)])
                p.add("vector",
                      (lambda m_: lambda e: e.scalar_tensor_tensor(
                          Sin[:, C // 2:C], self.xch[1][:, 0:T], m_,
                          Sin[:, C // 2:C], ALU.mult, ALU.add))(ms),
                      waits=[("dre", ldsh2)])
            v_sin = p.v("ve")

            Scb2 = [self.wsl[:, 4480:4608], self.wsl[:, 4608:4736]]
            x2bfp = [self.x2bf, self.x2bf2]
            hload = [0, 0]
            oload = [0, 0]
            hfree_ve = [v_sin, v_sin]
            hfree_ac = [0, 0]
            corr_mm = [0, 0]
            mm_stat = 0
            dob_all = p.v("dob")
            for hp in range(NHEAD // 2):
                for hl0 in range(2):
                    h = 2 * hp + hl0
                    oload[h % 2] = self.dma("sync",
                                  (lambda h_, o_: lambda e: e.dma_start(
                                      out=o_[:], in_=self.ob_h[h_ * 128:(h_ + 1) * 128, :]))(h, ohb[h % 2]),
                                  "dob2", waits=[("ve", hfree_ve[h % 2]),
                                                 ("ac", hfree_ac[h % 2]),
                                                 ("dob", dob_all)])
                    hload[h % 2] = self.dma("sync",
                                  (lambda h_, q_: lambda e: e.dma_start(
                                      out=q_[:], in_=self.qh_h[h_ * 128:(h_ + 1) * 128, :]))(h, qhb[h % 2]),
                                  "dob3", waits=[("dob", dob_all)])
                for Bk in range(NB):
                    for hl in range(2):
                        h = 2 * hp + hl
                        qb_ = qhb[hl]
                        ob_ = ohb[hl]
                        hs = slice(h * 128, (h + 1) * 128)
                        p.add("vector",
                              (lambda h_, Bk_, hs_, hl_: lambda e: e.tensor_scalar(
                                  Scb2[hl_][:], Sin[:, hs_],
                                  Lcum[:, h_ * 8 + Bk_:h_ * 8 + Bk_ + 1], None,
                                  ALU.mult, ALU.bypass))(h, Bk, hs, hl),
                              waits=[("ve", v_sin), ("mm", corr_mm[hl]),
                                     ("dob3", hload[hl]), ("dob2", oload[hl])])
                        p.add("vector",
                              (lambda q_, h_, Bk_: lambda e: e.tensor_scalar(
                                  q_[:, Bk_ * 128 + 64:Bk_ * 128 + 128],
                                  q_[:, Bk_ * 128 + 64:Bk_ * 128 + 128],
                                  Lc0[:, h_ * 8 + Bk_:h_ * 8 + Bk_ + 1], None,
                                  ALU.mult, ALU.bypass))(qb_, h, Bk))
                        v_sc = p.v("ve")
                        mm_c = p.add("tensor",
                              (lambda q_, Bk_, hl_: lambda e: e.matmul(
                                  PB[hl_][:, 0:128], Scb2[hl_][:],
                                  q_[:, Bk_ * 128:(Bk_ + 1) * 128],
                                  start=True, stop=True))(qb_, Bk, hl),
                              waits=[("ve", v_sc)],
                              inc=("mm", 1))
                        corr_mm[hl] = mm_c
                        p.add("vector",
                              (lambda o_, Bk_, hl_: lambda e: e.tensor_add(
                                  o_[:, Bk_ * 128:(Bk_ + 1) * 128],
                                  o_[:, Bk_ * 128:(Bk_ + 1) * 128],
                                  PB[hl_][:, 0:128]))(ob_, Bk, hl),
                              waits=[("mm", mm_c)])
                for hl0 in range(2):
                    h = 2 * hp + hl0
                    ob_ = ohb[h % 2]
                    p.add("scalar",
                          (lambda o_, hl_: lambda e: e.activation(
                              x2bfp[hl_][:, 0:T], o_[:], AF.Square))(ob_, h % 2),
                          waits=[("ve", p.v("ve"))])
                    v_sq = p.v("ac")
                    for half in range(2):
                        mm_stat = p.add("tensor",
                              (lambda h_, half_, hl_: lambda e: e.matmul(
                                  PB[2 + half_][:], self.ones_bf[:],
                                  x2bfp[hl_][:, half_ * 512:(half_ + 1) * 512],
                                  start=(h_ == 0), stop=(h_ == NHEAD - 1)))(h, half, h % 2),
                              waits=[("ac", v_sq), ("ve", self.v_ones)],
                              inc=("mm", 1) if (h == NHEAD - 1 and half == 1) else None) \
                              or mm_stat
                    hfree_ac[h % 2] = v_sq
                    hfree_ve[h % 2] = p.v("ve")
                    self.dma("sync",
                             (lambda h_, o_: lambda e: e.dma_start(
                                 out=self.ob_h[h_ * 128:(h_ + 1) * 128, :], in_=o_[:]))(h, ob_),
                             "dob", waits=[("ac", v_sq)])

            epssl = self.vec_sb[:, self.VC["eps"]:self.VC["eps"] + 1]
            p.add("vector", lambda e: e.tensor_scalar(
                self.r_tile[:, 0:512], PB[2][:], 1.0 / C, None, ALU.mult, ALU.bypass),
                  waits=[("mm", mm_stat)])
            p.add("vector", lambda e: e.tensor_scalar(
                self.r_tile[:, 512:1024], PB[3][:], 1.0 / C, None, ALU.mult, ALU.bypass))
            v_ms = p.v("ve")
            p.add("scalar", lambda e: e.activation(
                self.r_tile[:, 0:T], self.r_tile[:, 0:T], AF.Ln, bias=epssl),
                  waits=[("ve", v_ms)])
            p.add("scalar", lambda e: e.activation(
                self.r_tile[:, 0:T], self.r_tile[:, 0:T], AF.Exp, scale=-0.5))
            v_rms = p.v("ac")

            on = [E(f"on{a}", (128, T), BF16) for a in range(NCH)]
            ldo = [0, 0]
            for h in range(NCH):
                ob_ = ohb[h % 2]
                ldo[h % 2] = self.dma("sync",
                              (lambda h_, o_: lambda e: e.dma_start(
                                  out=o_[:], in_=self.ob_h[h_ * 128:(h_ + 1) * 128, :]))(h, ob_),
                              "dob",
                              waits=[("ve", p.v("ve")) if h < 2 else ("ve", on_done[h - 2])])
                gsl_ = v3[:, h:h + 1]
                p.add("vector",
                      (lambda h_, o_, g_: lambda e: e.scalar_tensor_tensor(
                          on[h_][:], o_[:], g_, self.r_tile[:, 0:T],
                          ALU.mult, ALU.mult))(h, ob_, gsl_),
                      waits=[("ac", v_rms), ("dob", ldo[h % 2]), ("dvl", v_cld)])
                if h == 0:
                    on_done = [0] * NCH
                on_done[h] = p.v("ve")
            v_on = p.v("ve")

            res_st = [0] * NCH

            def rhs_o1(a, half):
                return on[a][:, half * 512:(half + 1) * 512]

            def evac_o1(j, b0, mm_done):
                xb = self.xch[j % 2]
                xsem = "dxa" if j % 2 == 0 else "dxb"
                xrl = self.dma("sync",
                               (lambda jj, b_: lambda e: e.dma_start(
                                   out=b_[:, 0:T],
                                   in_=self.resid_b[jj * 128:(jj + 1) * 128, :]))(j, xb),
                               xsem,
                               waits=[("dst", res_st[j - 2])] if j >= 2 else
                               [("ve", p.v("ve")), ("ac", p.v("ac"))])
                for half in range(2):
                    p.add("vector",
                          (lambda pb_, b_, h_: lambda e: e.tensor_add(
                              b_[:, h_ * 512:(h_ + 1) * 512],
                              b_[:, h_ * 512:(h_ + 1) * 512], pb_[:]))(
                              PB[b0 + half], xb, half),
                          waits=[("mm", mm_done), (xsem, xrl)] if half == 0 else ())
                vv = p.v("ve")
                res_st[j] = self.dma("sync",
                                     (lambda jj, b_: lambda e: e.dma_start(
                                         out=self.resid_a[jj * 128:(jj + 1) * 128, :],
                                         in_=b_[:, 0:T]))(j, xb),
                                     "dst", waits=[("ve", vv)])
                return (vv, p.v("ac"))

            self.emit_proj(self.wo1_p, NCH, NCH, rhs_o1, evac_o1, v_on)


    def l1f(self):
        nc, p = self.nc, self.p
        PB = self.PB
        V4 = {"ws_up": 0, "ws_gate": 64, "ws_dp": 128, "su_ug": 144,
              "su_dp": 160}
        C_SQ = 2048 ** -0.5
        I_SQ = 8192 ** -0.5

        with ExitStack() as sc:
            E = lambda n, s, d: self.sb(n, s, d, sc)
            pb7 = sc.enter_context(nc.psum_tensor("pb7L", [128, 512], F32))
            PBL = PB + [pb7]
            xt = [E(f"xt{a}", (128, T), BF16) for a in range(NCH)]
            v4 = E("v4sb", (128, 256), F32)
            sup = E("sup", (128, NHI), F32)
            sgt = E("sgt", (128, NHI), F32)
            sdn = E("sdn", (128, NCH), F32)
            h128 = E("h128sb", (128, 128), BF16)

            ldv = self.dma("sync", lambda e: e.dma_start(out=v4[:], in_=self.vecs4_d[:]), "dvl")
            ldh = self.dma("sync", lambda e: e.dma_start(out=h128[:], in_=self.h128_d[:]), "dvl")
            p.add("vector", lambda e: e.tensor_scalar(
                sup[:], v4[:, V4["ws_up"]:V4["ws_up"] + NHI], C_SQ, None,
                ALU.mult, ALU.bypass), waits=[("dvl", ldv)])
            p.add("vector", lambda e: e.tensor_mul(
                sup[:], sup[:], v4[:, V4["su_dp"]:V4["su_dp"] + NHI]))
            p.add("vector", lambda e: e.tensor_scalar(
                sgt[:], v4[:, V4["ws_gate"]:V4["ws_gate"] + NHI], C_SQ, None,
                ALU.mult, ALU.bypass))
            p.add("vector", lambda e: e.tensor_scalar(
                sdn[:], v4[:, V4["ws_dp"]:V4["ws_dp"] + NCH], I_SQ, None,
                ALU.mult, ALU.bypass))
            v_sv = p.v("ve")

            with ExitStack() as scA:
                self.alloc_scratch(scA, big=False)
                y4 = [self.sb(f"y4_{a}", (128, T), BF16, scA) for a in range(NCH)]
                hsu = self.sb("hsu", (128, 2048), BF16, scA)
                hsn = self.sb("hsn", (128, 2048), BF16, scA)

                for a in range(NCH):
                    p.add("vector",
                          (lambda aa: lambda e: e.tensor_scalar(
                              hsu[:, aa * 128:(aa + 1) * 128], h128[:],
                              v4[:, V4["su_ug"] + aa:V4["su_ug"] + aa + 1], None,
                              ALU.mult, ALU.bypass))(a),
                          waits=[("dvl", ldh)])
                    p.add("vector",
                          (lambda aa: lambda e: e.tensor_scalar(
                              hsn[:, aa * 128:(aa + 1) * 128],
                              hsu[:, aa * 128:(aa + 1) * 128], -1.0, None,
                              ALU.mult, ALU.bypass))(a))
                v_hsu = p.v("ve")

                v_y4 = self.emit_ln(self.resid_a, T, y4)

                H16s = hadamard(16)
                efree = [(0, 0)] * 2
                for i in range(NCH):
                    b0 = 0 if i % 2 == 0 else 4
                    mm_a = 0
                    for half in range(2):
                        for j in range(NCH):
                            tile = (hsu if H16s[i, j] > 0 else hsn)
                            mm_a = p.add("tensor",
                                  (lambda pb_, t_, jj, h_, f_, l_: lambda e: e.matmul(
                                      pb_[:], t_[:, jj * 128:(jj + 1) * 128],
                                      y4[jj][:, h_ * 512:(h_ + 1) * 512],
                                      start=f_, stop=l_))(
                                      PBL[b0 + half], tile, j, half,
                                      j == 0, j == NCH - 1),
                                  waits=[("ve", max(v_y4, v_hsu)),
                                         ("ac", efree[i % 2][1])] if j == 0 else (),
                                  inc=("mm", 1) if (half == 1 and j == NCH - 1) else None) or mm_a
                    for half in range(2):
                        p.add("scalar",
                              (lambda pb_, ii, h_: lambda e: e.activation(
                                  xt[ii][:, h_ * 512:(h_ + 1) * 512], pb_[:],
                                  AF.Copy))(PBL[b0 + half], i, half),
                              waits=[("mm", mm_a)] if half == 0 else ())
                    efree[i % 2] = (0, p.v("ac"))
                v_fw1 = p.v("ac")

            xm2 = [E(f"xm2_{a}", (128, T), BF16) for a in range(NHI)]
            gsl = [self.wsl[:, i * C:(i + 1) * C] for i in range(2)]
            usl = [self.wsl[:, (2 + i) * C:(3 + i) * C] for i in range(2)]
            mm0 = p.v("mm")
            slab_user = [mm0] * 4
            xm_done = [0] * NHI
            sg_done = [0] * NHI
            for j in range(NHI):
                gb, ub = gsl[j % 2], usl[j % 2]
                wsem = "dwa" if j % 2 == 0 else "dwb"
                wgd = self.dma("gpsimd",
                               (lambda jj, s_: lambda e: e.dma_start(
                                   out=s_, in_=self.wgt_p[jj].rearrange("p a q -> p (a q)")))(j, gb),
                               wsem, waits=[("mm", slab_user[j % 2])])
                wud = self.dma("gpsimd",
                               (lambda jj, s_: lambda e: e.dma_start(
                                   out=s_, in_=self.wup_p[jj].rearrange("p a q -> p (a q)")))(j, ub),
                               wsem, waits=[("mm", slab_user[2 + j % 2])])
                gb0 = 0 if j % 2 == 0 else 4
                ub0 = 2 if j % 2 == 0 else 6
                bw = []
                if j >= 2:
                    bw = [("ac", sg_done[j - 2]), ("ve", xm_done[j - 2])]
                mmg = mmu = 0
                for (pb0, wbuf, wwait, si) in ((gb0, gb, wgd, j % 2),
                                               (ub0, ub, wud, 2 + j % 2)):
                    for half in range(2):
                        pb = PBL[pb0 + half]
                        for a in range(NCH):
                            r = p.add("tensor",
                                  (lambda pb_, l_, r_, f_, st_: lambda e: e.matmul(
                                      pb_[:], l_, r_, start=f_, stop=st_))(
                                      pb, wbuf[:, a * 128:(a + 1) * 128],
                                      xt[a][:, half * 512:(half + 1) * 512],
                                      a == 0, a == NCH - 1),
                                  waits=([(wsem, wwait), ("ac", v_fw1)] + bw)
                                  if a == 0 else (),
                                  inc=("mm", 1) if (a == NCH - 1 and half == 1) else None)
                            if a == NCH - 1 and half == 1:
                                slab_user[si] = r
                mm_g = p.v("mm") - 1
                mm_u = p.v("mm")
                sgl = sgt[:, j:j + 1]
                sul = sup[:, j:j + 1]
                sbuf_g = self.xch[j % 2]
                for half in range(2):
                    p.add("scalar",
                          (lambda pb_, dst, s_: lambda e: e.activation(
                              dst, pb_[:], AF.Silu, scale=s_))(
                              PBL[gb0 + half],
                              sbuf_g[:, half * 512:(half + 1) * 512], sgl),
                          waits=[("mm", mm_g), ("ve", v_sv)] if half == 0 else ())
                sg_done[j] = p.v("ac")
                for half in range(2):
                    p.add("vector",
                          (lambda pb_, jj, h_, s_, sg_: lambda e: e.scalar_tensor_tensor(
                              xm2[jj][:, h_ * 512:(h_ + 1) * 512], pb_[:], s_,
                              sg_[:, h_ * 512:(h_ + 1) * 512],
                              ALU.mult, ALU.mult))(
                              PBL[ub0 + half], j, half, sul, sbuf_g),
                          waits=[("mm", mm_u), ("ac", sg_done[j])] if half == 0 else ())
                xm_done[j] = p.v("ve")
            v_xm2 = p.v("ve")

            ef2 = [(v_xm2, p.v("ac"))] * 2
            for a in range(NHI):
                b0 = 0 if a % 2 == 0 else 2
                mm_a = 0
                for half in range(2):
                    mm_a = p.add("tensor",
                          (lambda pb_, aa, h_: lambda e: e.matmul(
                              pb_[:], h128[:],
                              xm2[aa][:, h_ * 512:(h_ + 1) * 512],
                              start=True, stop=True))(PBL[b0 + half], a, half),
                          waits=[("ve", v_xm2), ("ac", ef2[a % 2][1]),
                                 ("ve", ef2[a % 2][0])] if half == 0 else (),
                          inc=("mm", 1) if half == 1 else None) or mm_a
                for half in range(2):
                    p.add("scalar",
                          (lambda pb_, aa, h_: lambda e: e.activation(
                              xm2[aa][:, h_ * 512:(h_ + 1) * 512], pb_[:],
                              AF.Copy))(PBL[b0 + half], a, half),
                          waits=[("mm", mm_a)] if half == 0 else ())
                ef2[a % 2] = (p.v("ve"), p.v("ac"))
            v_A2 = p.v("ac")

            v_fw2 = (p.v("ve"), p.v("po"))
            ksl2 = [self.wsl[:, i * 4096:(i + 1) * 4096] for i in range(2)]
            res_st = [0] * NCH
            slab_mm = [p.v("mm")] * 2
            bfree2 = [(p.v("ve"), p.v("ac"))] * 2
            for j in range(NCH):
                b0 = 0 if j % 2 == 0 else 4
                mm_done = 0
                for kh in range(2):
                    sl = ksl2[kh]
                    wsem = "dna" if kh == 0 else "dnb"
                    wd = self.dma("sync",
                                  (lambda jj, kh_, s_: lambda e: e.dma_start(
                                      out=s_,
                                      in_=self.wdn2_h[jj, :, kh_ * 32:(kh_ + 1) * 32, :]
                                      .rearrange("p a q -> p (a q)")))(j, kh, sl),
                                  wsem, waits=[("mm", slab_mm[kh])])
                    for half in range(2):
                        pb = PBL[b0 + half]
                        bw = [("ve", bfree2[j % 2][0]), ("ac", bfree2[j % 2][1])]
                        for a2 in range(32):
                            a = kh * 32 + a2
                            is_last = (kh == 1 and half == 1 and a2 == 31)
                            r = p.add("tensor",
                                  (lambda pb_, l_, r_, f_, st_: lambda e: e.matmul(
                                      pb_[:], l_, r_, start=f_, stop=st_))(
                                      pb, sl[:, a2 * 128:(a2 + 1) * 128],
                                      xm2[a][:, half * 512:(half + 1) * 512],
                                      (kh == 0 and a2 == 0), (kh == 1 and a2 == 31)),
                                  waits=([(wsem, wd), ("ac", v_A2)] + bw)
                                  if a2 == 0 else (),
                                  inc=("mm", 1) if (half == 1 and a2 == 31) else None)
                            if half == 1 and a2 == 31:
                                slab_mm[kh] = r
                                mm_done = r
                xb = self.xch[j % 2]
                xsem = "dxa" if j % 2 == 0 else "dxb"
                xrl = self.dma("sync",
                               (lambda jj, b_: lambda e: e.dma_start(
                                   out=b_[:, 0:T],
                                   in_=self.resid_a[jj * 128:(jj + 1) * 128, :]))(j, xb),
                               xsem,
                               waits=[("dst", res_st[j - 2])] if j >= 2 else
                               [("ve", p.v("ve")), ("ac", p.v("ac"))])
                sdl = sdn[:, j:j + 1]
                for half in range(2):
                    p.add("vector",
                          (lambda pb_, b_, h_, s_: lambda e: e.scalar_tensor_tensor(
                              b_[:, h_ * 512:(h_ + 1) * 512], pb_[:], s_,
                              b_[:, h_ * 512:(h_ + 1) * 512],
                              ALU.mult, ALU.add))(PBL[b0 + half], xb, half, sdl),
                          waits=[("mm", mm_done), (xsem, xrl)] if half == 0 else ())
                vv = p.v("ve")
                bfree2[j % 2] = (vv, p.v("ac"))
                res_st[j] = self.dma("sync",
                                     (lambda jj, b_: lambda e: e.dma_start(
                                         out=self.resid_b[jj * 128:(jj + 1) * 128, :],
                                         in_=b_[:, 0:T]))(j, xb),
                                     "dst", waits=[("ve", vv)])

    def emit_proj_arena(self, wpack, nk, nm, rhs_fn, evac_fn, v_rhs_ready,
                        arena, extra_k=None):
        p = self.p
        PB = self.PB
        kcols = nk * 128
        slabs = [arena[:, i * kcols:(i + 1) * kcols] for i in range(2)]
        mm0 = p.v("mm")
        slab_user = [mm0, mm0]
        bank_free = [(0, 0)] * 2
        wdone = [0] * nm
        for j in range(nm):
            sl = slabs[j % 2]
            wsem = "dwa" if j % 2 == 0 else "dwb"
            wdone[j] = self.dma("gpsimd",
                                (lambda jj, s_: lambda e: e.dma_start(
                                    out=s_, in_=wpack[jj].rearrange("p a q -> p (a q)")))(j, sl),
                                wsem, waits=[("mm", slab_user[j % 2])])
            b0 = 0 if j % 2 == 0 else 4
            bw = []
            if j >= 2:
                bw = [("ve", bank_free[j % 2][0]), ("ac", bank_free[j % 2][1])]
            mm_done = 0
            for half in range(2):
                pb = PB[b0 + half]
                for a in range(nk):
                    is_last = (half == 1 and a == nk - 1 and extra_k is None)
                    mm_done = p.add("tensor",
                          (lambda pb_, l_, r_, f_, st_: lambda e: e.matmul(
                              pb_[:], l_, r_, start=f_, stop=st_))(
                              pb, sl[:, a * 128:(a + 1) * 128], rhs_fn(a, half),
                              a == 0, (a == nk - 1) and extra_k is None),
                          waits=([(wsem, wdone[j]), ("ve", v_rhs_ready)] + bw)
                          if a == 0 else (),
                          inc=("mm", 1) if is_last else None) or mm_done
                if extra_k is not None:
                    mm_done = extra_k(j, pb, half, half == 1) or mm_done
            slab_user[j % 2] = mm_done
            bank_free[j % 2] = evac_fn(j, b0, mm_done)


def prep_inputs(inputs, core, n_sub=4):
    b, h = core // 2, core % 2
    t0 = h * T
    x = inputs["L0_attn_x"]
    xt = np.ascontiguousarray(x[b, t0:t0 + T, :].T)
    if h == 1:
        bcol = np.ascontiguousarray(x[b, t0 - 1, :])[:, None].astype(np.float32)
    else:
        bcol = np.zeros((C, 1), np.float32)
    x_fm = np.ascontiguousarray(np.concatenate([bcol, xt], axis=1))

    vecs = np.zeros((128, 128), np.float32)
    vecs[:, 0:16] = vec_fm(inputs["L0_attn_time_decay"])
    vecs[:, 16:32] = vec_fm(inputs["L0_attn_time_first"])
    vecs[:, 32:48] = vec_fm(inputs["L0_attn_time_mix_k"])
    vecs[:, 48:64] = vec_fm(inputs["L0_attn_time_mix_v"])
    vecs[:, 64:80] = vec_fm(inputs["L0_attn_time_mix_r"])
    vecs[:, 80] = 1e-5

    jramp1 = np.broadcast_to(
        np.arange(0, T, dtype=np.float32)[None, :], (128, T)).copy()
    msel = np.zeros((128, 8), np.float32)
    if h == 1:
        msel[:, core - 1] = 1.0

    def u8(w):
        return np.ascontiguousarray(w).view(np.uint8).reshape(
            w.shape[0], w.shape[1], 4)[..., 0]

    vecs4 = np.zeros((128, 256), np.float32)
    vecs4[:, 0:64] = vec_fm(inputs["L1_ffn_Wscale_ug"][:8192])
    vecs4[:, 64:128] = vec_fm(inputs["L1_ffn_Wscale_ug"][8192:])
    vecs4[:, 128:144] = vec_fm(inputs["L1_ffn_Wscale_dp"])
    vecs4[:, 144:160] = vec_fm(inputs["L1_ffn_SU_ug"])
    vecs4[:, 160:224] = vec_fm(inputs["L1_ffn_SU_dp"])

    vecs2 = np.zeros((128, 128), np.float32)
    vecs2[:, 0:44] = vec_fm(inputs["L0_ffn_s_gate"])
    vecs2[:, 44:88] = vec_fm(inputs["L0_ffn_s_up"])
    vecs2[:, 88:104] = vec_fm(inputs["L0_ffn_s_down"])

    return {
        "x_fm": x_fm,
        "wk_p": pack_lhsT(np.ascontiguousarray(inputs["L0_attn_Wk"].T)),
        "wv_p": pack_lhsT(np.ascontiguousarray(inputs["L0_attn_Wv"].T)),
        "wr_p": pack_lhsT(np.ascontiguousarray(inputs["L0_attn_Wr"].T)),
        "wo_p": pack_lhsT(np.ascontiguousarray(inputs["L0_attn_Wo"].T)),
        "wg_p": pack_lhsT(np.ascontiguousarray(u8(inputs["L0_ffn_wq_gate"]).T)),
        "wu_p": pack_lhsT(np.ascontiguousarray(u8(inputs["L0_ffn_wq_up"]).T)),
        "wd_p": pack_lhsT(np.ascontiguousarray(u8(inputs["L0_ffn_wq_down"]).T)),
        "wup_p": pack_lhsT(np.ascontiguousarray(inputs["L1_ffn_W_up"].T)),
        "wgt_p": pack_lhsT(np.ascontiguousarray(inputs["L1_ffn_W_gate"].T)),
        "wdn_z": np.ascontiguousarray(
            inputs["L1_ffn_W_down"].reshape(2048, 64, 64, 2)
            .transpose(1, 3, 2, 0).reshape(128, 64, 2048)),
        "h64i2": h64i2(),
        "h128": hadamard(128).astype(BF),
        "vecs4": vecs4,
        "wq1_p": pack_lhsT(np.ascontiguousarray(inputs["L1_attn_Wq"])),
        "wo1_p": pack_lhsT(np.ascontiguousarray(inputs["L1_attn_Wo"])),
        "wf_p": pack_rhs(np.ascontiguousarray(inputs["L1_attn_Wf"]), 256),
        "wi_p": pack_rhs(np.ascontiguousarray(inputs["L1_attn_Wi"]), 256),
        "vecs3": vec_fm(inputs["L1_attn_g_weight"]).copy(),
        "ident": np.eye(128, dtype=np.float32).astype(BF),
        "u64bd": u64bd(), "mask64": mask64(),
        "vecs": vecs, "vecs2": vecs2, "jramp1": jramp1, "msel": msel,
    }


def h64i2():
    H = hadamard(64)
    out = np.zeros((128, 128), np.float32)
    for k in range(64):
        for pb in range(2):
            for j in range(64):
                out[k * 2 + pb, pb * 64 + j] = H[k, j]
    return out.astype(BF)


def u64bd():
    u = np.zeros((128, 128), np.float32)
    for i in range(128):
        for j in range(128):
            if i <= j and i // 64 == j // 64:
                u[i, j] = 1.0
    return u.astype(BF)


def mask64():
    m = np.zeros((128, 64), np.float32)
    for i in range(128):
        for j in range(64):
            if (i % 64) <= j:
                m[i, j] = 1.0
    return m.astype(BF)


def kernel(n_sub=4, **inputs):
    inputs = {k: np.asarray(v) for k, v in inputs.items()}
    mk = MK(n_sub=n_sub)
    nc = mk.build()
    in_maps = [prep_inputs(inputs, c, n_sub) for c in range(NCORES)]
    res = run_bass_kernel_spmd(nc, in_maps, core_ids=list(range(NCORES)))
    out = np.zeros((B, 2 * T, C), np.float32)
    for c in range(NCORES):
        b, h = c // 2, c % 2
        out[b, h * T:(h + 1) * T, :] = res.results[c]["out_fm"].T
    return out
